# version 01
# baseline (speedup 1.0000x reference)
import sys
from contextlib import ExitStack

sys.path.insert(0, "/opt/trn_rl_repo")
import numpy as np
import ml_dtypes
import concourse.bass as bass
import concourse.mybir as mybir
from concourse.bass_utils import run_bass_kernel_spmd

F32 = mybir.dt.float32
BF16 = mybir.dt.bfloat16
AF = mybir.ActivationFunctionType
ALU = mybir.AluOpType

D = 1024
HID = 4096
FQK = 64
TCORE = 2048
TT = 512
NTILE = TCORE // TT
CH = 128
NCHT = TT // CH
NCH = TCORE // CH
DC = D // 128
HC = HID // 128
LN_EPS = 1e-5
ATTN_EPS = 1e-6

ENG = ("pe", "act", "dve", "gps", "sp")
SECT = dict(pe="tensor", act="scalar", dve="vector", gps="gpsimd", sp="sync")
QSEM = dict(sp="qsp", act="qact", gps="qgps")
SEMS = ("pe", "act", "dve", "gps", "qsp", "qc", "qw0", "qw1", "qact", "qgps", "cc")


class P:
    def __init__(self):
        self.ops = {e: [] for e in ENG}
        self.cnt = {s: 0 for s in SEMS}
        self.waited = {e: {s: 0 for s in SEMS} for e in ENG}
        self.last = {e: None for e in ENG}
        self.maxthresh = {s: 0 for s in SEMS}

    def _wait(self, eng, tok):
        if tok is None:
            return
        sem, val = tok
        if val > self.maxthresh[sem]:
            self.maxthresh[sem] = val
        if val <= self.waited[eng][sem]:
            return
        self.waited[eng][sem] = val
        self.ops[eng].append(("w", sem, val))

    def do(self, eng, fn, deps=(), inc=True, dma=False, sem=None):
        if not dma and eng in ("dve", "act", "gps"):
            self._wait(eng, self.last[eng])
        if dma:
            qs = QSEM[eng] if dma is True else dma
            self._wait(eng, (qs, self.maxthresh[qs]))
        for d in deps:
            self._wait(eng, d)
        if dma:
            s, n = qs, 16
        elif sem is not None:
            s, n = sem, 1
        elif inc:
            s, n = eng, 1
        else:
            s, n = None, 0
        tok = None
        if s is not None:
            self.cnt[s] += n
            tok = (s, self.cnt[s])
            if s == eng:
                self.last[eng] = tok
        self.ops[eng].append(("x", fn, s, n))
        return tok

    def barrier(self):
        snap = [(s, v) for s, v in self.cnt.items() if v > 0]
        for e in ENG:
            for s, v in snap:
                self._wait(e, (s, v))

    def replay(self, block, sems):
        for eng in ENG:
            section = getattr(block, SECT[eng])
            lst = self.ops[eng]

            def body(e, lst=lst):
                for item in lst:
                    if item[0] == "w":
                        e.wait_ge(sems[item[1]], item[2])
                    else:
                        ins = item[1](e)
                        if item[2] is not None:
                            ins.then_inc(sems[item[2]], item[3])

            section(body)


def build(stages=("P", "A", "B", "C")):
    nc = bass.Bass("TRN2", target_bir_lowering=False, debug=False)

    def din(name, shape, dt=F32):
        return nc.dram_tensor(name, shape, dt, kind="ExternalInput")

    def dint(name, shape, dt=F32):
        return nc.dram_tensor(name, shape, dt, kind="Internal")

    x_in = din("x", [D, TCORE])
    wf = {}
    for name, nm, nk in [("w1", HC, DC), ("w2", HC, DC), ("h1", HC, HC),
                         ("h2", HC, HC), ("w3", DC, HC),
                         ("vw", DC, DC), ("ow", DC, DC),
                         ("f1", HC, DC), ("f2", DC, HC)]:
        wf[name] = din(name + "f", [nm * 128, nk * 128])
    wf["qw"] = din("qwf", [128, DC * FQK])
    wf["kw"] = din("kwf", [128, DC * FQK])
    scd = {}
    for name, w in [("s1", HC), ("s2", HC), ("sh1", HC), ("sh2", HC), ("f1b", HC),
                    ("s3", DC), ("vb", DC), ("ob", DC), ("f2b", DC),
                    ("ln1w", DC), ("ln1b", DC), ("ln2w", DC), ("ln2b", DC)]:
        scd[name] = din(name + "_t", [128, w])
    scd["qb"] = din("qb_t", [FQK, 1])
    scd["kb"] = din("kb_t", [FQK, 1])
    ident_d = din("ident", [128, 128], BF16)
    maskT_d = din("maskT", [128, 128], BF16)
    ones32_d = din("ones32", [128, 128])
    onesbf_d = din("onesbf", [128, 128], BF16)
    smask_d = din("smask", [FQK, 1])
    rmask_d = din("rmask", [FQK, 1])

    y_out = nc.dram_tensor("y", [D, TCORE], F32, kind="ExternalOutput")

    xres = dint("xres", [D, TCORE])
    wh = {k: dint(k + "hh", list(wf[k].shape), BF16) for k in wf}
    wl = {k: dint(k + "ll", list(wf[k].shape), BF16)
          for k in ("vw", "ow", "f1", "f2", "qw", "kw")}
    cc_in = dint("cc_in", [FQK, 1032])
    cc_out = dint("cc_out", [FQK, 1032])

    p = P()

    with ExitStack() as top:
        sems = {s: top.enter_context(nc.semaphore("s_" + s)) for s in SEMS}
        SBt = lambda n, s, d=F32: top.enter_context(nc.sbuf_tensor(n, s, d))

        ident = SBt("c_ident", [128, 128], BF16)
        maskT = SBt("c_maskT", [128, 128], BF16)
        ones32 = SBt("c_ones32", [128, 128])
        onesbf = SBt("c_onesbf", [128, 128], BF16)
        scs = {k: SBt("c_" + k, list(v.shape)) for k, v in scd.items()}
        smask = SBt("c_smask", [FQK, 1])
        rmask = SBt("c_rmask", [FQK, 1])
        s3h = SBt("c_s3h", [128, DC])
        qbn = SBt("c_qbn", [FQK, 1])
        kbn = SBt("c_kbn", [FQK, 1])
        ones64 = SBt("c_ones64", [FQK, 128])

        for sb_t, d_t in ([(ident, ident_d), (maskT, maskT_d), (ones32, ones32_d),
                           (onesbf, onesbf_d), (smask, smask_d), (rmask, rmask_d)]
                          + [(scs[k], scd[k]) for k in scs]):
            cload = p.do("sp", lambda e, a=sb_t, b=d_t: e.dma_start(a[:], b.ap()), dma="qc")
        p.do("dve", lambda e: e.tensor_scalar_mul(s3h[:], scs["s3"][:], 0.5), deps=[cload])
        p.do("dve", lambda e: e.tensor_scalar_mul(qbn[:], scs["qb"][:], -1.0), deps=[cload])
        p.do("dve", lambda e: e.tensor_scalar_mul(kbn[:], scs["kb"][:], -1.0), deps=[cload])
        const_tok = p.do("dve", lambda e: e.memset(ones64[:], 1.0))

        if "P" in stages:
            with ExitStack() as ps_:
                pb = lambda n, s, d=F32: ps_.enter_context(nc.sbuf_tensor(n, s, d))
                st32 = [pb(f"p_st{i}", [128, 4096]) for i in range(2)]
                hib = [pb(f"p_hi{i}", [128, 4096], BF16) for i in range(2)]
                lob = [pb(f"p_lo{i}", [128, 4096], BF16) for i in range(2)]
                tmpb = [pb(f"p_tm{i}", [128, 4096]) for i in range(2)]
                for name in ("w1", "w2", "h1", "h2", "w3"):
                    rows, cols = wf[name].shape
                    rb = max(128, (4 * 1024 * 1024) // (4 * cols))
                    for r in range(0, rows, rb):
                        r2 = min(rows, r + rb)
                        p.do("gps", lambda e, o=wh[name], i=wf[name], a=r, b=r2:
                             e.dma_start(o.ap()[a:b, :], i.ap()[a:b, :]), dma=True)
                par_tok = [None, None]
                for name in ("vw", "ow", "f1", "f2", "qw", "kw"):
                    rows, cols = wf[name].shape
                    cb = min(cols, 4096)
                    blk = 0
                    for r in range(0, rows, 128):
                        for c0 in range(0, cols, cb):
                            c1 = min(cols, c0 + cb)
                            i = blk % 2
                            blk += 1
                            n = c1 - c0
                            ld = p.do("sp", lambda e, o=st32[i], ii=wf[name], r=r, a=c0, b=c1, n=n:
                                      e.dma_start(o[:, 0:n], ii.ap()[r:r + 128, a:b]),
                                      deps=[par_tok[0], par_tok[1]], dma=True)
                            h1_ = p.do("dve", lambda e, o=hib[i], ii=st32[i], n=n:
                                       e.tensor_copy(o[:, 0:n], ii[:, 0:n]), deps=[ld])
                            d1 = p.do("dve", lambda e, o=tmpb[i], ii=st32[i], h=hib[i], n=n:
                                      e.tensor_sub(o[:, 0:n], ii[:, 0:n], h[:, 0:n]), deps=[h1_])
                            l1 = p.do("dve", lambda e, o=lob[i], ii=tmpb[i], n=n:
                                      e.tensor_copy(o[:, 0:n], ii[:, 0:n]), deps=[d1])
                            p.do("act", lambda e, o=wh[name], ii=hib[i], r=r, a=c0, b=c1, n=n:
                                 e.dma_start(o.ap()[r:r + 128, a:b], ii[:, 0:n]),
                                 deps=[h1_], dma=True)
                            sl = p.do("act", lambda e, o=wl[name], ii=lob[i], r=r, a=c0, b=c1, n=n:
                                      e.dma_start(o.ap()[r:r + 128, a:b], ii[:, 0:n]),
                                      deps=[l1], dma=True)
                            par_tok[i] = sl
            p.barrier()

        def emit_ln_stats(bufs, xsrc_fn, ps, deps_extra, stats_read):
            sq, tmp, mu, ex2, var, veps, sqb, y0, rb_ = bufs
            sq_free = [None, None]
            mm1 = mm2 = None
            for c in range(DC):
                i = c % 2
                stok = p.do("act", lambda e, c=c, i=i:
                            e.activation(sq[i][:], xsrc_fn(c), AF.Square, bias=0.0, scale=1.0),
                            deps=[sq_free[i]] + deps_extra(c))
                mm1 = p.do("pe", lambda e, c=c:
                           e.matmul(ps[:, 0:TT], ones32[:], xsrc_fn(c), start=(c == 0), stop=(c == DC - 1)),
                           deps=[stats_read] + deps_extra(c), inc=True)
                mm2 = p.do("pe", lambda e, c=c, i=i:
                           e.matmul(ps[:, TT:2 * TT], ones32[:], sq[i][:], start=(c == 0), stop=(c == DC - 1)),
                           deps=[stok], inc=True)
                sq_free[i] = mm2
            t_mu = p.do("dve", lambda e: e.tensor_scalar_mul(mu[:], ps[:, 0:TT], 1.0 / D), deps=[mm1])
            t_e2 = p.do("dve", lambda e: e.tensor_scalar_mul(ex2[:], ps[:, TT:2 * TT], 1.0 / D), deps=[mm2])
            p.do("dve", lambda e: e.tensor_tensor(tmp[0][:], mu[:], mu[:], ALU.mult), deps=[t_mu])
            t_v = p.do("dve", lambda e: e.tensor_tensor(var[:], ex2[:], tmp[0][:], ALU.subtract), deps=[t_e2])
            t_ve = p.do("dve", lambda e: e.tensor_scalar(veps[:], var[:], 1.0, LN_EPS, ALU.mult, ALU.add), deps=[t_v])
            t_s = p.do("act", lambda e: e.activation(sqb[:], veps[:], AF.Sqrt, bias=0.0, scale=1.0), deps=[t_ve])
            t_y0 = p.do("dve", lambda e: e.reciprocal(y0[:], sqb[:]), deps=[t_s])
            p.do("dve", lambda e: e.tensor_tensor(tmp[0][:], y0[:], y0[:], ALU.mult), deps=[t_y0])
            p.do("dve", lambda e: e.tensor_tensor(tmp[1][:], veps[:], tmp[0][:], ALU.mult))
            p.do("dve", lambda e: e.tensor_scalar(tmp[0][:], tmp[1][:], -0.5, 1.5, ALU.mult, ALU.add))
            t_r = p.do("dve", lambda e: e.tensor_tensor(rb_[:], y0[:], tmp[0][:], ALU.mult))
            return t_mu, t_r, t_v, t_e2

        def emit_ffn(src, dst, tag):
            with ExitStack() as fs:
                fb = lambda n, s, d=F32: fs.enter_context(nc.sbuf_tensor(tag + n, s, d))
                xt = fb("xt", [128, DC * TT])
                th = fb("th", [128, DC * TT], BF16)
                tl = fb("tl", [128, DC * TT], BF16)
                hh = fb("hh", [128, HC * TT], BF16)
                hl = fb("hl", [128, HC * TT], BF16)
                gh = fb("gh", [128, HC * TT], BF16)
                gl = fb("gl", [128, HC * TT], BF16)
                sq = [fb(f"sq{i}", [128, TT]) for i in range(2)]
                tmp = [fb(f"tm{i}", [128, TT]) for i in range(2)]
                mu = fb("mu", [128, TT]); ex2 = fb("ex2", [128, TT])
                var = fb("var", [128, TT]); veps = fb("veps", [128, TT])
                sqb = ex2; y0 = fb("y0", [128, TT])
                rb_ = fb("rb", [128, TT])
                silu = [fb(f"si{i}", [128, TT]) for i in range(2)]
                gel = silu
                xo = [fb(f"xo{i}", [128, TT]) for i in range(2)]
                wsl = [fb(f"ws{i}", [128, 4096], BF16) for i in range(2)]
                pst = fs.enter_context(nc.psum_tensor(tag + "pst", [128, 1024], F32))
                pmm = [fs.enter_context(nc.psum_tensor(f"{tag}pm{i}", [128, TT], F32)) for i in range(4)]
                lnbufs = (sq, tmp, mu, ex2, var, veps, sqb, y0, rb_)

                prev_pe = [None]
                prev_dve = [None]
                stats_read = [None]
                xo_free = [None, None]
                slot_free = [None, None]
                bank_free = [None] * 4
                silu_free = [None, None]
                gel_free = [None, None]

                def group(bank, slot, passes, nk, deps):
                    g_ = None
                    d0 = list(deps)
                    total = nk * len(passes)
                    idx = 0
                    for k in range(nk):
                        for (wo, mv) in passes:
                            idx += 1
                            g_ = p.do("pe", lambda e, k=k, wo=wo, mv=mv, slot=slot, bank=bank,
                                      st=(idx == 1), sp=(idx == total):
                                      e.matmul(pmm[bank][:], wsl[slot][:, wo + k * 128:wo + (k + 1) * 128],
                                               mv[:, k * TT:(k + 1) * TT], start=st, stop=sp),
                                      deps=d0, inc=(idx == total))
                            d0 = ()
                    return g_

                for tau in range(NTILE):
                    xtok = []
                    for c in range(DC):
                        tok = p.do("sp", lambda e, c=c, tau=tau:
                                   e.dma_start(xt[:, c * TT:(c + 1) * TT],
                                               src.ap()[c * 128:(c + 1) * 128, tau * TT:(tau + 1) * TT]),
                                   deps=[prev_pe[0], prev_dve[0]], dma=True)
                        xtok.append(tok)
                    t_mu, t_r, t_v, _ = emit_ln_stats(
                        lnbufs, lambda c: xt[:, c * TT:(c + 1) * TT], pst,
                        lambda c: [xtok[-1]], stats_read[0])
                    stats_read[0] = t_v
                    t_tok = None
                    for c in range(DC):
                        a = p.do("dve", lambda e, c=c: e.tensor_tensor(tmp[0][:], xt[:, c * TT:(c + 1) * TT], mu[:], ALU.subtract),
                                 deps=[t_mu, t_r, xtok[-1], prev_pe[0]])
                        b = p.do("dve", lambda e: e.tensor_tensor(tmp[1][:], tmp[0][:], rb_[:], ALU.mult), deps=[a])
                        h_ = p.do("dve", lambda e, c=c: e.tensor_copy(th[:, c * TT:(c + 1) * TT], tmp[1][:]), deps=[b])
                        d_ = p.do("dve", lambda e, c=c: e.tensor_sub(tmp[0][:], tmp[1][:], th[:, c * TT:(c + 1) * TT]), deps=[h_])
                        t_tok = p.do("dve", lambda e, c=c: e.tensor_copy(tl[:, c * TT:(c + 1) * TT], tmp[0][:]), deps=[d_])

                    h_done = None
                    for m in range(HC):
                        slot = m % 2
                        p.do("sp", lambda e, m=m, slot=slot:
                             e.dma_start(wsl[slot][:, 0:DC * 128], wh["w1"].ap()[m * 128:(m + 1) * 128, :]),
                             deps=[slot_free[slot]], dma="qw%d" % slot)
                        wtok = p.do("sp", lambda e, m=m, slot=slot:
                                    e.dma_start(wsl[slot][:, DC * 128:2 * DC * 128], wh["w2"].ap()[m * 128:(m + 1) * 128, :]),
                                    dma="qw%d" % slot)
                        bA, bB = (2 * m) % 4, (2 * m + 1) % 4
                        gA = group(bA, slot, [(0, th), (0, tl)], DC, [wtok, t_tok, bank_free[bA]])
                        gB = group(bB, slot, [(DC * 128, th), (DC * 128, tl)], DC, [bank_free[bB]])
                        slot_free[slot] = gB
                        si = m % 2
                        a1 = p.do("act", lambda e, m=m, si=si, bA=bA:
                                  e.activation(silu[si][:], pmm[bA][:], AF.Silu, bias=0.0,
                                               scale=scs["s1"][:, m:m + 1]),
                                  deps=[gA, silu_free[si]])
                        d1 = p.do("dve", lambda e, m=m, si=si, bB=bB:
                                  e.scalar_tensor_tensor(tmp[0][:], pmm[bB][:], scs["s2"][:, m:m + 1],
                                                         silu[si][:], ALU.mult, ALU.mult),
                                  deps=[gB, a1])
                        silu_free[si] = d1
                        h_ = p.do("dve", lambda e, m=m: e.tensor_copy(hh[:, m * TT:(m + 1) * TT], tmp[0][:]), deps=[d1])
                        d2 = p.do("dve", lambda e, m=m: e.tensor_sub(tmp[1][:], tmp[0][:], hh[:, m * TT:(m + 1) * TT]), deps=[h_])
                        h_done = p.do("dve", lambda e, m=m: e.tensor_copy(hl[:, m * TT:(m + 1) * TT], tmp[1][:]), deps=[d2])
                        bank_free[bA] = a1
                        bank_free[bB] = d1

                    def hidden(wname, sname, src_h, src_l, dst_h, dst_l, ready):
                        done = ready
                        for m in range(HC):
                            slot = m % 2
                            wtok = p.do("sp", lambda e, m=m, slot=slot, w=wh[wname]:
                                        e.dma_start(wsl[slot][:, 0:HC * 128], w.ap()[m * 128:(m + 1) * 128, :]),
                                        deps=[slot_free[slot]], dma="qw%d" % slot)
                            bank = m % 4
                            g_ = group(bank, slot, [(0, src_h), (0, src_l)], HC,
                                       [wtok, ready, bank_free[bank]])
                            slot_free[slot] = g_
                            gi = m % 2
                            a1 = p.do("act", lambda e, m=m, gi=gi, bank=bank, sname=sname:
                                      e.activation(gel[gi][:], pmm[bank][:], AF.Gelu, bias=0.0,
                                                   scale=scs[sname][:, m:m + 1]),
                                      deps=[g_, gel_free[gi]])
                            h_ = p.do("dve", lambda e, m=m, gi=gi, dh=dst_h:
                                      e.tensor_copy(dh[:, m * TT:(m + 1) * TT], gel[gi][:]), deps=[a1])
                            d_ = p.do("dve", lambda e, m=m, gi=gi, dh=dst_h:
                                      e.tensor_sub(tmp[1][:], gel[gi][:], dh[:, m * TT:(m + 1) * TT]), deps=[h_])
                            done = p.do("dve", lambda e, m=m, dl=dst_l:
                                        e.tensor_copy(dl[:, m * TT:(m + 1) * TT], tmp[1][:]), deps=[d_])
                            gel_free[gi] = d_
                            bank_free[bank] = a1
                        return done

                    g_done = hidden("h1", "sh1", hh, hl, gh, gl, h_done)
                    f_done = hidden("h2", "sh2", gh, gl, hh, hl, g_done)

                    last_pe = None
                    last_dve = None
                    for m in range(DC):
                        slot = m % 2
                        wtok = p.do("sp", lambda e, m=m, slot=slot:
                                    e.dma_start(wsl[slot][:, 0:HC * 128], wh["w3"].ap()[m * 128:(m + 1) * 128, :]),
                                    deps=[slot_free[slot]], dma="qw%d" % slot)
                        bank = m % 4
                        g_ = group(bank, slot, [(0, hh), (0, hl)], HC, [wtok, f_done, bank_free[bank]])
                        slot_free[slot] = g_
                        xi = m % 2
                        s_ = p.do("dve", lambda e, m=m, xi=xi, bank=bank:
                                  e.scalar_tensor_tensor(xo[xi][:], pmm[bank][:], s3h[:, m:m + 1],
                                                         xt[:, m * TT:(m + 1) * TT], ALU.mult, ALU.add),
                                  deps=[g_, xo_free[0], xo_free[1], const_tok])
                        st_ = p.do("act", lambda e, m=m, tau=tau, xi=xi:
                                   e.dma_start(dst.ap()[m * 128:(m + 1) * 128, tau * TT:(tau + 1) * TT], xo[xi][:]),
                                   deps=[s_], dma=True)
                        xo_free[xi] = st_
                        bank_free[bank] = s_
                        last_pe = g_
                        last_dve = s_
                    prev_pe[0] = last_pe
                    prev_dve[0] = last_dve

        def emit_attn(src, dst):
            with ExitStack() as bso:
                bb = lambda n, s, d=F32: bso.enter_context(nc.sbuf_tensor("b_" + n, s, d))
                numb = bb("num", [128, DC * TCORE])
                mid = bso.enter_context(ExitStack())
                bm = lambda n, s, d=F32: mid.enter_context(nc.sbuf_tensor("b_" + n, s, d))
                qhi = bm("qhi", [FQK, TCORE], BF16)
                qlo = bm("qlo", [FQK, TCORE], BF16)
                mu_all = bm("mua", [128, TCORE])
                r0_all = bm("r0a", [128, TCORE])
                den_all = bm("dena", [128, TCORE])
                kvr = bm("kvr", [FQK, 1032])
                kvrh = bm("kvrh", [FQK, 1024], BF16)
                kvrl = bm("kvrl", [FQK, 1024], BF16)
                ksrh = bm("ksrh", [FQK, 128], BF16)
                ksrl = bm("ksrl", [FQK, 128], BF16)
                kstmp = bm("kstmp", [FQK, 256])

                def phase_b1():
                    with ExitStack() as b1:
                        b1b = lambda n, s, d=F32: b1.enter_context(nc.sbuf_tensor("b1_" + n, s, d))
                        xt = b1b("xt", [128, DC * TT])
                        nh = b1b("nh", [128, DC * TT], BF16)
                        nl = b1b("nl", [128, DC * TT], BF16)
                        vh = b1b("vh", [128, DC * TT], BF16)
                        vl = b1b("vl", [128, DC * TT], BF16)
                        kh = b1b("kh", [FQK, TT], BF16)
                        kl = b1b("kl", [FQK, TT], BF16)
                        qf = b1b("qf", [FQK, TT])
                        kf = qf
                        sq = [b1b(f"sq{i}", [128, TT]) for i in range(2)]
                        tmp = [b1b(f"tm{i}", [128, TT]) for i in range(2)]
                        mu = b1b("mu", [128, TT]); ex2 = b1b("ex2", [128, TT])
                        var = b1b("var", [128, TT]); veps = b1b("veps", [128, TT])
                        sqb = ex2; y0 = b1b("y0", [128, TT])
                        rb_ = b1b("rb", [128, TT]); rc_ = b1b("rc", [128, TT])
                        aeps = veps
                        vth = b1b("vth", [128, 1024], BF16)
                        vtl = b1b("vtl", [128, 1024], BF16)
                        kts = b1b("kts", [128, 128], BF16)
                        smf = b1b("smf", [128, 128])
                        smh = b1b("smh", [128, 128], BF16)
                        sml = b1b("sml", [128, 128], BF16)
                        kvf = [b1b(f"kvf{i}", [FQK, 1032]) for i in range(2)]
                        kvbh = b1b("kvbh", [FQK, 1024], BF16)
                        kvbl = b1b("kvbl", [FQK, 1024], BF16)
                        ksr = [b1b(f"ksr{i}", [FQK, 128]) for i in range(2)]
                        ksi = b1b("ksi", [FQK, 1])
                        krh = b1b("krh", [FQK, 128], BF16)
                        krl = b1b("krl", [FQK, 128], BF16)
                        wsl = [b1b("ws0", [128, 1024], BF16), b1b("ws1", [128, 2048], BF16)]
                        pnum = b1.enter_context(nc.psum_tensor("b1_pnum", [128, 1024], F32))
                        pkv = b1.enter_context(nc.psum_tensor("b1_pkv", [FQK, 1024], F32))
                        pvth = b1.enter_context(nc.psum_tensor("b1_pvth", [128, 1024], BF16))
                        pvtl = b1.enter_context(nc.psum_tensor("b1_pvtl", [128, 1024], BF16))
                        pmsf = b1.enter_context(nc.psum_tensor("b1_pmsf", [128, 256], F32))
                        pmsb = b1.enter_context(nc.psum_tensor("b1_pmsb", [128, 128], BF16))
                        lnbufs = (sq, tmp, mu, ex2, var, veps, sqb, y0, rb_)

                        prev_pe = [None]; prev_dve = [None]; prev_act = [None]
                        stats_read = [None]
                        slot_free = [None, None]
                        pnum_free = [None]
                        vt_rd = [None, None]
                        vt_ps = [None, None]
                        kt_rd = [None]; kt_ps = [None]
                        sm_rd = [None]
                        sp_ps = [None]
                        den_ps = [None]
                        kv_evac = [None]
                        kv_tok = [None]
                        ksr_tok = [None, None]
                        kr_free = [None]
                        z = p.do("dve", lambda e: e.memset(ksr[1][:], 0.0))
                        ksr_tok[1] = z
                        p.do("dve", lambda e: e.memset(kvf[1][:], 0.0))
                        p.do("dve", lambda e: e.memset(kvf[0][:], 0.0))
                        gnum = gkv = gden = ks2 = ekt = ne = None

                        for tau in range(NTILE):
                            xtok = []
                            for c in range(DC):
                                tok = p.do("sp", lambda e, c=c, tau=tau:
                                           e.dma_start(xt[:, c * TT:(c + 1) * TT],
                                                       src.ap()[c * 128:(c + 1) * 128, tau * TT:(tau + 1) * TT]),
                                           deps=[prev_pe[0], prev_dve[0], prev_act[0]], dma=True)
                                xtok.append(tok)
                            t_mu, t_r, t_v, _ = emit_ln_stats(
                                lnbufs, lambda c: xt[:, c * TT:(c + 1) * TT], pnum,
                                lambda c: [xtok[-1], pnum_free[0]], stats_read[0])
                            stats_read[0] = t_v
                            p.do("dve", lambda e, tau=tau: e.tensor_copy(mu_all[:, tau * TT:(tau + 1) * TT], mu[:]), deps=[t_mu])
                            t_r0s = p.do("dve", lambda e, tau=tau: e.tensor_copy(r0_all[:, tau * TT:(tau + 1) * TT], rb_[:]), deps=[t_r])
                            p.do("dve", lambda e: e.tensor_tensor(tmp[0][:], rb_[:], rb_[:], ALU.mult), deps=[t_r])
                            p.do("dve", lambda e: e.tensor_tensor(tmp[1][:], var[:], tmp[0][:], ALU.mult))
                            c3 = p.do("dve", lambda e: e.tensor_scalar(aeps[:], tmp[1][:], 1.0, LN_EPS, ALU.mult, ALU.add))
                            c4 = p.do("act", lambda e: e.activation(sqb[:], aeps[:], AF.Sqrt, bias=0.0, scale=1.0), deps=[c3])
                            c5 = p.do("dve", lambda e: e.reciprocal(y0[:], sqb[:]), deps=[c4])
                            p.do("dve", lambda e: e.tensor_tensor(tmp[0][:], y0[:], y0[:], ALU.mult), deps=[c5])
                            p.do("dve", lambda e: e.tensor_tensor(tmp[1][:], aeps[:], tmp[0][:], ALU.mult))
                            p.do("dve", lambda e: e.tensor_scalar(tmp[0][:], tmp[1][:], -0.5, 1.5, ALU.mult, ALU.add))
                            c9 = p.do("dve", lambda e: e.tensor_tensor(rc_[:], y0[:], tmp[0][:], ALU.mult))
                            t_rr = p.do("dve", lambda e: e.tensor_tensor(rc_[:], rb_[:], rc_[:], ALU.mult), deps=[c9, t_r0s])
                            del c9
                            n_tok = None
                            for c in range(DC):
                                a = p.do("dve", lambda e, c=c: e.tensor_tensor(tmp[0][:], xt[:, c * TT:(c + 1) * TT], mu[:], ALU.subtract),
                                         deps=[t_mu, t_rr, xtok[-1], prev_pe[0]])
                                b = p.do("dve", lambda e: e.tensor_tensor(tmp[1][:], tmp[0][:], rc_[:], ALU.mult), deps=[a])
                                b2 = p.do("dve", lambda e, c=c: e.tensor_scalar(tmp[0][:], tmp[1][:], scs["ln1w"][:, c:c + 1],
                                                                                scs["ln1b"][:, c:c + 1], ALU.mult, ALU.add), deps=[b])
                                h_ = p.do("dve", lambda e, c=c: e.tensor_copy(nh[:, c * TT:(c + 1) * TT], tmp[0][:]), deps=[b2])
                                d_ = p.do("dve", lambda e, c=c: e.tensor_sub(tmp[1][:], tmp[0][:], nh[:, c * TT:(c + 1) * TT]), deps=[h_])
                                n_tok = p.do("dve", lambda e, c=c: e.tensor_copy(nl[:, c * TT:(c + 1) * TT], tmp[1][:]), deps=[d_])

                            def qk_proj(wname, bias_sb, bias_neg, out_f, oh_ap, ol_ap, psl, extra):
                                wtok1 = p.do("sp", lambda e, w=wh[wname]: e.dma_start(wsl[0][:, 0:DC * FQK], w.ap()[:, :]),
                                             deps=[slot_free[0]], dma="qw0")
                                wtok2 = p.do("sp", lambda e, w=wl[wname]: e.dma_start(wsl[0][:, DC * FQK:2 * DC * FQK], w.ap()[:, :]),
                                             dma="qw0")
                                g_ = None
                                d0 = [wtok2, n_tok, pnum_free[0]] + list(extra)
                                idx = 0
                                for k in range(DC):
                                    for wo, mv in ((0, nh), (0, nl), (DC * FQK, nh)):
                                        idx += 1
                                        il = idx == 3 * DC
                                        g_ = p.do("pe", lambda e, k=k, wo=wo, mv=mv, psl=psl, st=(idx == 1), sp=il:
                                                  e.matmul(pnum[0:FQK, psl * TT:(psl + 1) * TT],
                                                           wsl[0][:, wo + k * FQK:wo + (k + 1) * FQK],
                                                           mv[:, k * TT:(k + 1) * TT], start=st, stop=sp),
                                                  deps=d0, inc=il)
                                        d0 = ()
                                slot_free[0] = g_
                                e1 = p.do("act", lambda e, psl=psl, bb_=bias_sb: e.activation(tmp[0][0:FQK, :], pnum[0:FQK, psl * TT:(psl + 1) * TT],
                                                                                              AF.Relu, bias=bb_[:], scale=1.0),
                                          deps=[g_, n_tok, prev_act[0]] + list(extra))
                                e2 = p.do("act", lambda e, psl=psl, bb_=bias_neg: e.activation(tmp[1][0:FQK, :], pnum[0:FQK, psl * TT:(psl + 1) * TT],
                                                                                               AF.Relu, bias=bb_[:], scale=-1.0),
                                          deps=[g_, n_tok, const_tok])
                                e3 = p.do("act", lambda e: e.activation(tmp[1][0:FQK, :], tmp[1][0:FQK, :], AF.Exp, bias=0.0, scale=-1.0), deps=[e2])
                                f_ = p.do("dve", lambda e, o=out_f: e.tensor_tensor(o[:], tmp[0][0:FQK, :], tmp[1][0:FQK, :], ALU.add),
                                          deps=[e1, e3, prev_pe[0]])
                                h_ = p.do("dve", lambda e, o=oh_ap, f=out_f: e.tensor_copy(o, f[:]), deps=[f_])
                                d_ = p.do("dve", lambda e, o=out_f, hq=oh_ap: e.tensor_sub(tmp[0][0:FQK, :], o[:], hq), deps=[h_])
                                l_ = p.do("dve", lambda e, o=ol_ap: e.tensor_copy(o, tmp[0][0:FQK, :]), deps=[d_])
                                return g_, e2, l_

                            gq, e2q, q_tok = qk_proj("qw", scs["qb"], qbn, qf,
                                                     qhi[:, tau * TT:(tau + 1) * TT],
                                                     qlo[:, tau * TT:(tau + 1) * TT], 0, [])
                            gk, e2k, k_tok = qk_proj("kw", scs["kb"], kbn, kf, kh[:], kl[:], 1, [q_tok])
                            v_tok = None
                            vg = [e2q, e2k]
                            for m in range(DC):
                                wtok1 = p.do("sp", lambda e, m=m, w=wh["vw"]:
                                             e.dma_start(wsl[1][:, 0:DC * 128], w.ap()[m * 128:(m + 1) * 128, :]),
                                             deps=[slot_free[1]], dma="qw1")
                                wtok2 = p.do("sp", lambda e, m=m, w=wl["vw"]:
                                             e.dma_start(wsl[1][:, DC * 128:2 * DC * 128], w.ap()[m * 128:(m + 1) * 128, :]),
                                             dma="qw1")
                                psl = m % 2
                                g_ = None
                                d0 = [wtok2, n_tok, vg[psl]]
                                idx = 0
                                for k in range(DC):
                                    for wo, mv in ((0, nh), (0, nl), (DC * 128, nh)):
                                        idx += 1
                                        il = idx == 3 * DC
                                        g_ = p.do("pe", lambda e, k=k, wo=wo, mv=mv, psl=psl, st=(idx == 1), sp=il:
                                                  e.matmul(pnum[:, psl * TT:(psl + 1) * TT],
                                                           wsl[1][:, wo + k * 128:wo + (k + 1) * 128],
                                                           mv[:, k * TT:(k + 1) * TT], start=st, stop=sp),
                                                  deps=d0, inc=il)
                                        d0 = ()
                                slot_free[1] = g_
                                s1_ = p.do("dve", lambda e, m=m, psl=psl:
                                           e.tensor_scalar_add(tmp[0][:], pnum[:, psl * TT:(psl + 1) * TT], scs["vb"][:, m:m + 1]),
                                           deps=[g_, prev_pe[0]])
                                h_ = p.do("dve", lambda e, m=m: e.tensor_copy(vh[:, m * TT:(m + 1) * TT], tmp[0][:]), deps=[s1_])
                                d_ = p.do("dve", lambda e, m=m: e.tensor_sub(tmp[1][:], tmp[0][:], vh[:, m * TT:(m + 1) * TT]), deps=[h_])
                                v_tok = p.do("dve", lambda e, m=m: e.tensor_copy(vl[:, m * TT:(m + 1) * TT], tmp[1][:]), deps=[d_])
                                vg[psl] = s1_
                            pnum_free[0] = v_tok

                            for cc_ in range(NCHT):
                                g = tau * NCHT + cc_
                                cur, prv = g % 2, (g + 1) % 2
                                tvh = None
                                for j in range(DC):
                                    tvh = p.do("pe", lambda e, j=j, cc_=cc_:
                                               e.transpose(pvth[:, j * 128:(j + 1) * 128],
                                                           vh[:, j * TT + cc_ * CH: j * TT + (cc_ + 1) * CH], ident[:]),
                                               deps=[v_tok, vt_ps[0]], inc=(j == DC - 1))
                                evh = p.do("act", lambda e: e.copy(vth[:], pvth[:]), deps=[tvh, vt_rd[0]])
                                vt_ps[0] = evh
                                tvl = None
                                for j in range(DC):
                                    tvl = p.do("pe", lambda e, j=j, cc_=cc_:
                                               e.transpose(pvtl[:, j * 128:(j + 1) * 128],
                                                           vl[:, j * TT + cc_ * CH: j * TT + (cc_ + 1) * CH], ident[:]),
                                               deps=[v_tok, vt_ps[1]], inc=(j == DC - 1))
                                evl = p.do("act", lambda e: e.copy(vtl[:], pvtl[:]), deps=[tvl, vt_rd[1]])
                                vt_ps[1] = evl
                                p.do("pe", lambda e, cc_=cc_: e.transpose(pmsb[0:CH, 0:FQK], kh[:, cc_ * CH:(cc_ + 1) * CH], ident[0:FQK, 0:FQK]),
                                     deps=[k_tok, kt_ps[0]], inc=False)
                                tk2 = p.do("pe", lambda e, cc_=cc_: e.transpose(pmsb[0:CH, FQK:2 * FQK], kl[:, cc_ * CH:(cc_ + 1) * CH], ident[0:FQK, 0:FQK]),
                                           inc=True)
                                ekt = p.do("act", lambda e: e.copy(kts[:], pmsb[:]), deps=[tk2, kt_rd[0]])
                                kt_ps[0] = ekt
                                p.do("pe", lambda e, cc_=cc_, g=g: e.matmul(pmsf[:, 0:128], kh[:, cc_ * CH:(cc_ + 1) * CH],
                                                                            qhi[:, g * CH:(g + 1) * CH], start=True, stop=False),
                                     deps=[q_tok, k_tok, sp_ps[0]], inc=False)
                                p.do("pe", lambda e, cc_=cc_, g=g: e.matmul(pmsf[:, 0:128], kh[:, cc_ * CH:(cc_ + 1) * CH],
                                                                            qlo[:, g * CH:(g + 1) * CH], start=False, stop=False),
                                     inc=False)
                                sp3 = p.do("pe", lambda e, cc_=cc_, g=g: e.matmul(pmsf[:, 0:128], kl[:, cc_ * CH:(cc_ + 1) * CH],
                                                                                  qhi[:, g * CH:(g + 1) * CH], start=False, stop=True),
                                           inc=True)
                                m1 = p.do("dve", lambda e: e.tensor_tensor(smf[:], pmsf[:, 0:128], maskT[:], ALU.mult),
                                          deps=[sp3, sm_rd[0]])
                                m2 = p.do("dve", lambda e: e.tensor_copy(smh[:], smf[:]), deps=[m1])
                                m3 = p.do("dve", lambda e: e.tensor_sub(smf[:], smf[:], smh[:]), deps=[m2])
                                m4 = p.do("dve", lambda e: e.tensor_copy(sml[:], smf[:]), deps=[m3])
                                sp_ps[0] = m1
                                kr1 = p.do("dve", lambda e, prv=prv: e.tensor_copy(krh[:], ksr[prv][:]),
                                           deps=[ksr_tok[prv], kr_free[0]])
                                kr2 = p.do("dve", lambda e, prv=prv: e.tensor_sub(kstmp[:, 0:128], ksr[prv][:], krh[:]), deps=[kr1])
                                kr3 = p.do("dve", lambda e: e.tensor_copy(krl[:], kstmp[:, 0:128]), deps=[kr2])
                                d0 = [m4, evh, evl, pnum_free[0], kv_tok[0], stats_read[0]]
                                nmm = 6 if g > 0 else 3
                                idx = 0
                                for j in range(DC):
                                    mml = [(vth, smh, None), (vth, sml, None), (vtl, smh, None)]
                                    if g > 0:
                                        mml += [(kvbh, qhi, g), (kvbh, qlo, g), (kvbl, qhi, g)]
                                    for t_i, (lh, rh, qg) in enumerate(mml):
                                        idx += 1
                                        il = idx == nmm * DC
                                        if qg is None:
                                            gnum = p.do("pe", lambda e, j=j, lh=lh, rh=rh, st=(t_i == 0), sp=(t_i == len(mml) - 1):
                                                        e.matmul(pnum[:, j * 128:(j + 1) * 128], lh[:, j * 128:(j + 1) * 128],
                                                                 rh[:], start=st, stop=sp),
                                                        deps=d0, inc=il)
                                        else:
                                            gnum = p.do("pe", lambda e, j=j, lh=lh, rh=rh, g=g, sp=(t_i == len(mml) - 1):
                                                        e.matmul(pnum[:, j * 128:(j + 1) * 128], lh[:, j * 128:(j + 1) * 128],
                                                                 rh[:, g * CH:(g + 1) * CH], start=False, stop=sp),
                                                        deps=d0, inc=il)
                                        d0 = ()
                                d0 = [m4, den_ps[0], kr3]
                                dml = [(onesbf, smh, None, True), (onesbf, sml, None, False)]
                                if g > 0:
                                    dml += [(krh, qhi, g, False), (krh, qlo, g, False), (krl, qhi, g, False)]
                                for t_i, (lh, rh, qg, st_) in enumerate(dml):
                                    il = t_i == len(dml) - 1
                                    if qg is None:
                                        gden = p.do("pe", lambda e, lh=lh, rh=rh, st=st_, sp=il:
                                                    e.matmul(pmsf[:, 128:256], lh[:], rh[:], start=st, stop=sp),
                                                    deps=d0, inc=il)
                                    else:
                                        gden = p.do("pe", lambda e, lh=lh, rh=rh, g=g, sp=il:
                                                    e.matmul(pmsf[:, 128:256], lh[:], rh[:, g * CH:(g + 1) * CH],
                                                             start=False, stop=sp),
                                                    deps=d0, inc=il)
                                    d0 = ()
                                kr_free[0] = gden
                                de = p.do("dve", lambda e, g=g: e.tensor_scalar(den_all[:, g * CH:(g + 1) * CH], pmsf[:, 128:256],
                                                                                1.0, ATTN_EPS, ALU.mult, ALU.add), deps=[gden])
                                den_ps[0] = de
                                ne = None
                                for j in range(DC):
                                    ne = p.do("dve", lambda e, j=j, g=g:
                                              e.tensor_copy(numb[:, j * TCORE + g * CH: j * TCORE + (g + 1) * CH],
                                                            pnum[:, j * 128:(j + 1) * 128]), deps=[gnum])
                                pnum_free[0] = ne
                                sm_rd[0] = gnum
                                vt_rd = [gnum, gnum]
                                d0 = [ekt, evh, evl, kv_evac[0]]
                                idx = 0
                                for (wo, mvv) in ((0, vth), (0, vtl), (FQK, vth)):
                                    for hf in range(2):
                                        idx += 1
                                        gkv = p.do("pe", lambda e, wo=wo, mvv=mvv, hf=hf,
                                                   st=(idx <= 2), sp=(idx >= 5):
                                                   e.matmul(pkv[:, hf * TT:(hf + 1) * TT], kts[:, wo:wo + FQK],
                                                            mvv[:, hf * TT:(hf + 1) * TT], start=st, stop=sp),
                                                   deps=d0, inc=(idx == 6))
                                        d0 = ()
                                kt_rd[0] = gkv
                                kv1 = p.do("dve", lambda e, cur=cur, prv=prv:
                                           e.tensor_tensor(kvf[cur][0:FQK, 0:1024], kvf[prv][0:FQK, 0:1024], pkv[:], ALU.add), deps=[gkv])
                                kv_evac[0] = kv1
                                kv2 = p.do("dve", lambda e, cur=cur: e.tensor_copy(kvbh[:], kvf[cur][0:FQK, 0:1024]), deps=[kv1, gnum])
                                kv3 = p.do("dve", lambda e, cur=cur, prv=prv: e.tensor_sub(kvf[prv][0:FQK, 0:1024], kvf[cur][0:FQK, 0:1024], kvbh[:]), deps=[kv2])
                                kv4 = p.do("dve", lambda e, prv=prv: e.tensor_copy(kvbl[:], kvf[prv][0:FQK, 0:1024]), deps=[kv3])
                                kv_tok[0] = kv4
                                ks1 = p.do("dve", lambda e, cc_=cc_: e.reduce_sum(ksi[:], kf[:, cc_ * CH:(cc_ + 1) * CH],
                                                                                  axis=mybir.AxisListType.X), deps=[k_tok])
                                ks2 = p.do("dve", lambda e, cur=cur, prv=prv: e.tensor_scalar_add(ksr[cur][:], ksr[prv][:], ksi[:]),
                                           deps=[ks1, ksr_tok[cur], kr3])
                                ksr_tok[cur] = ks2
                            prev_pe[0] = gkv
                            prev_dve[0] = ks2
                            prev_act[0] = ekt

                        lastp = (NCH - 1) % 2
                        ccs = xt
                        pk1 = p.do("dve", lambda e, lastp=lastp: e.tensor_copy(ccs[0:FQK, 0:1024], kvf[lastp][0:FQK, 0:1024]),
                                   deps=[kv_evac[0], kv_tok[0], prev_pe[0]])
                        pk2 = p.do("dve", lambda e, lastp=lastp: e.tensor_copy(ccs[0:FQK, 1024:1025], ksr[lastp][:, 0:1]),
                                   deps=[ksr_tok[lastp]])
                        pk3 = p.do("dve", lambda e: e.memset(ccs[0:FQK, 1025:1032], 0.0))
                        oth = (NCH) % 2
                        pk4 = p.do("dve", lambda e, oth=oth: e.tensor_scalar_mul(kvf[oth][0:FQK, 0:1032], ccs[0:FQK, 0:1032], smask[:]), deps=[pk1, pk2, pk3, const_tok])
                        sd = p.do("gps", lambda e, oth=oth: e.dma_start(cc_in.ap(), kvf[oth][0:FQK, 0:1032]), deps=[pk4], dma=True)
                        cct = p.do("gps", lambda e: e.collective_compute(
                            "AllReduce", ALU.add, replica_groups=[[0, 1], [2, 3], [4, 5], [6, 7]],
                            ins=[cc_in.ap()], outs=[cc_out.ap()]), deps=[sd], sem="cc")
                        rc1 = p.do("gps", lambda e: e.dma_start(kvr[:], cc_out.ap()), deps=[cct], dma=True)
                        rm1 = p.do("dve", lambda e: e.tensor_scalar_mul(kvr[:], kvr[:], rmask[:]), deps=[rc1, const_tok])
                        rm2 = p.do("dve", lambda e: e.tensor_copy(kvrh[:], kvr[:, 0:1024]), deps=[rm1])
                        rm3 = p.do("dve", lambda e: e.tensor_sub(kvf[0][0:FQK, 0:1024], kvr[:, 0:1024], kvrh[:]), deps=[rm2])
                        rm4 = p.do("dve", lambda e: e.tensor_copy(kvrl[:], kvf[0][0:FQK, 0:1024]), deps=[rm3])
                        rk1 = p.do("dve", lambda e: e.tensor_scalar_mul(kstmp[:, 0:128], ones64[:], kvr[:, 1024:1025]),
                                   deps=[rm1, const_tok])
                        rk2 = p.do("dve", lambda e: e.tensor_copy(ksrh[:], kstmp[:, 0:128]), deps=[rk1])
                        rk3 = p.do("dve", lambda e: e.tensor_sub(kstmp[:, 128:256], kstmp[:, 0:128], ksrh[:]), deps=[rk2])
                        rk4 = p.do("dve", lambda e: e.tensor_copy(ksrl[:], kstmp[:, 128:256]), deps=[rk3])
                        bdone = rm4
                        bdone2 = rk4
                        return bdone, bdone2
                bdone, bdone2 = phase_b1()
                p.barrier()

                def phase_b2a():
                    with ExitStack() as b2:
                        b2b = lambda n, s, d=F32: b2.enter_context(nc.sbuf_tensor("b2_" + n, s, d))
                        xt = b2b("xt", [128, DC * TT])
                        ut = b2b("ut", [128, DC * TT])
                        ath = b2b("ath", [128, DC * TT], BF16)
                        atl = b2b("atl", [128, DC * TT], BF16)
                        tmp = [b2b(f"tm{i}", [128, TT]) for i in range(3)]
                        den2 = b2b("den2", [128, TT])
                        rec = b2b("rec", [128, TT])
                        wsl = [b2b(f"ws{i}", [128, 2048], BF16) for i in range(2)]
                        pco = [b2.enter_context(nc.psum_tensor(f"b2_pc{i}", [128, TT], F32)) for i in range(4)]

                        prev_pe = [None]; prev_dve = [None]
                        slot_free = [None, None]
                        bank_free = [None] * 4
                        for tau in range(NTILE):
                            xtok = []
                            for c in range(DC):
                                tok = p.do("sp", lambda e, c=c, tau=tau:
                                           e.dma_start(xt[:, c * TT:(c + 1) * TT],
                                                       src.ap()[c * 128:(c + 1) * 128, tau * TT:(tau + 1) * TT]),
                                           deps=[prev_pe[0], prev_dve[0]], dma=True)
                                xtok.append(tok)
                            utok = []
                            for c in range(DC):
                                a = p.do("dve", lambda e, c=c, tau=tau:
                                         e.tensor_tensor(tmp[0][:], xt[:, c * TT:(c + 1) * TT],
                                                         mu_all[:, tau * TT:(tau + 1) * TT], ALU.subtract),
                                         deps=[xtok[-1], prev_pe[0]])
                                utok.append(p.do("dve", lambda e, c=c, tau=tau:
                                                 e.tensor_tensor(ut[:, c * TT:(c + 1) * TT], tmp[0][:],
                                                                 r0_all[:, tau * TT:(tau + 1) * TT], ALU.mult),
                                                 deps=[a]))
                            gd = None
                            d0 = [bdone, bdone2, bank_free[3]]
                            dml = [(ksrh, qhi), (ksrh, qlo), (ksrl, qhi)]
                            for i_, (wv, qv) in enumerate(dml):
                                gd = p.do("pe", lambda e, wv=wv, qv=qv, tau=tau, st=(i_ == 0), sp=(i_ == 2):
                                          e.matmul(pco[3][:], wv[:], qv[:, tau * TT:(tau + 1) * TT], start=st, stop=sp),
                                          deps=d0, inc=(i_ == 2))
                                d0 = ()
                            dd = p.do("dve", lambda e, tau=tau: e.tensor_tensor(den2[:], den_all[:, tau * TT:(tau + 1) * TT],
                                                                                pco[3][:], ALU.add), deps=[gd])
                            bank_free[3] = dd
                            rcp = p.do("dve", lambda e: e.reciprocal(rec[:], den2[:]), deps=[dd])
                            a_tok = None
                            for j in range(DC):
                                bank = j % 2
                                gc = None
                                d0 = [bdone, bank_free[bank]]
                                for i_, (wv, qv) in enumerate(((kvrh, qhi), (kvrh, qlo), (kvrl, qhi))):
                                    gc = p.do("pe", lambda e, j=j, wv=wv, qv=qv, tau=tau, bank=bank, st=(i_ == 0), sp=(i_ == 2):
                                              e.matmul(pco[bank][:], wv[:, j * 128:(j + 1) * 128],
                                                       qv[:, tau * TT:(tau + 1) * TT], start=st, stop=sp),
                                              deps=d0, inc=(i_ == 2))
                                    d0 = ()
                                s1_ = p.do("dve", lambda e, j=j, tau=tau, bank=bank:
                                           e.tensor_tensor(tmp[1][:], numb[:, j * TCORE + tau * TT: j * TCORE + (tau + 1) * TT],
                                                           pco[bank][:], ALU.add), deps=[gc])
                                s2_ = p.do("dve", lambda e: e.tensor_tensor(tmp[2][:], tmp[1][:], rec[:], ALU.mult), deps=[s1_, rcp])
                                h_ = p.do("dve", lambda e, j=j: e.tensor_copy(ath[:, j * TT:(j + 1) * TT], tmp[2][:]), deps=[s2_])
                                d_ = p.do("dve", lambda e, j=j: e.tensor_sub(tmp[1][:], tmp[2][:], ath[:, j * TT:(j + 1) * TT]), deps=[h_])
                                a_tok = p.do("dve", lambda e, j=j: e.tensor_copy(atl[:, j * TT:(j + 1) * TT], tmp[1][:]), deps=[d_])
                                bank_free[bank] = s1_
                            last_pe = None
                            last_dve = None
                            for m in range(DC):
                                slot = m % 2
                                wtok1 = p.do("sp", lambda e, m=m, slot=slot, w=wh["ow"]:
                                             e.dma_start(wsl[slot][:, 0:DC * 128], w.ap()[m * 128:(m + 1) * 128, :]),
                                             deps=[slot_free[slot]], dma="qw%d" % slot)
                                wtok2 = p.do("sp", lambda e, m=m, slot=slot, w=wl["ow"]:
                                             e.dma_start(wsl[slot][:, DC * 128:2 * DC * 128], w.ap()[m * 128:(m + 1) * 128, :]),
                                             dma="qw%d" % slot)
                                bank = 2 + (m % 2)
                                g_ = None
                                d0 = [wtok2, a_tok, bank_free[bank]]
                                idx = 0
                                for k in range(DC):
                                    for wo, mv in ((0, ath), (0, atl), (DC * 128, ath)):
                                        idx += 1
                                        il = idx == 3 * DC
                                        g_ = p.do("pe", lambda e, k=k, wo=wo, mv=mv, slot=slot, bank=bank, st=(idx == 1), sp=il:
                                                  e.matmul(pco[bank][:], wsl[slot][:, wo + k * 128:wo + (k + 1) * 128],
                                                           mv[:, k * TT:(k + 1) * TT], start=st, stop=sp),
                                                  deps=d0, inc=il)
                                        d0 = ()
                                slot_free[slot] = g_
                                s_ = p.do("dve", lambda e, m=m, tau=tau, bank=bank:
                                          e.scalar_tensor_tensor(numb[:, m * TCORE + tau * TT: m * TCORE + (tau + 1) * TT],
                                                                 pco[bank][:], scs["ob"][:, m:m + 1],
                                                                 ut[:, m * TT:(m + 1) * TT], ALU.add, ALU.add),
                                          deps=[g_, utok[m]])
                                bank_free[bank] = s_
                                last_pe = g_
                                last_dve = s_
                            prev_pe[0] = last_pe
                            prev_dve[0] = last_dve
                    p.barrier()
                phase_b2a()
                mid.close()

                def phase_b2b():
                    with ExitStack() as b3:
                        b3b = lambda n, s, d=F32: b3.enter_context(nc.sbuf_tensor("b3_" + n, s, d))
                        xs2 = [b3b(f"xs{i}", [128, TT]) for i in range(2)]
                        n2h = b3b("n2h", [128, DC * TT], BF16)
                        n2l = b3b("n2l", [128, DC * TT], BF16)
                        ph = b3b("ph", [128, HC * TT], BF16)
                        pl = b3b("pl", [128, HC * TT], BF16)
                        sq = [b3b(f"sq{i}", [128, TT]) for i in range(2)]
                        tmp = [b3b(f"tm{i}", [128, TT]) for i in range(2)]
                        mu = b3b("mu", [128, TT]); ex2 = b3b("ex2", [128, TT])
                        var = b3b("var", [128, TT]); veps = b3b("veps", [128, TT])
                        sqb = ex2; y0 = b3b("y0", [128, TT])
                        rb_ = b3b("rb", [128, TT])
                        gel = [b3b("ge0", [128, TT])]
                        gel.append(gel[0])
                        xo = [b3b("xo0", [128, TT])]
                        xo.append(xo[0])
                        wsl = [b3b(f"ws{i}", [128, 8192], BF16) for i in range(2)]
                        pst = b3.enter_context(nc.psum_tensor("b3_pst", [128, 1024], F32))
                        pmm = [b3.enter_context(nc.psum_tensor(f"b3_pm{i}", [128, TT], F32)) for i in range(4)]
                        lnbufs = (sq, tmp, mu, ex2, var, veps, sqb, y0, rb_)

                        prev_pe = [None]; prev_dve = [None]
                        stats_read = [None]
                        slot_free = [None, None]
                        bank_free = [None] * 4
                        gel_free = [None, None]
                        xo_free = [None, None]
                        xs_free = [None, None]
                        for tau in range(NTILE):
                            t_mu, t_r, t_v, _ = emit_ln_stats(
                                lnbufs, lambda c, tau=tau: numb[:, c * TCORE + tau * TT: c * TCORE + (tau + 1) * TT],
                                pst, lambda c: [], stats_read[0])
                            stats_read[0] = t_v
                            n_tok = None
                            for c in range(DC):
                                a = p.do("dve", lambda e, c=c, tau=tau: e.tensor_tensor(tmp[0][:], numb[:, c * TCORE + tau * TT: c * TCORE + (tau + 1) * TT], mu[:], ALU.subtract),
                                         deps=[t_mu, t_r, prev_pe[0]])
                                b = p.do("dve", lambda e: e.tensor_tensor(tmp[1][:], tmp[0][:], rb_[:], ALU.mult), deps=[a])
                                b2_ = p.do("dve", lambda e, c=c: e.tensor_scalar(tmp[0][:], tmp[1][:], scs["ln2w"][:, c:c + 1],
                                                                                 scs["ln2b"][:, c:c + 1], ALU.mult, ALU.add), deps=[b])
                                h_ = p.do("dve", lambda e, c=c: e.tensor_copy(n2h[:, c * TT:(c + 1) * TT], tmp[0][:]), deps=[b2_])
                                d_ = p.do("dve", lambda e, c=c: e.tensor_sub(tmp[1][:], tmp[0][:], n2h[:, c * TT:(c + 1) * TT]), deps=[h_])
                                n_tok = p.do("dve", lambda e, c=c: e.tensor_copy(n2l[:, c * TT:(c + 1) * TT], tmp[1][:]), deps=[d_])

                            p_done = None
                            for m in range(HC):
                                slot = m % 2
                                wtok1 = p.do("sp", lambda e, m=m, slot=slot, w=wh["f1"]:
                                             e.dma_start(wsl[slot][:, 0:DC * 128], w.ap()[m * 128:(m + 1) * 128, :]),
                                             deps=[slot_free[slot]], dma="qw%d" % slot)
                                wtok2 = p.do("sp", lambda e, m=m, slot=slot, w=wl["f1"]:
                                             e.dma_start(wsl[slot][:, DC * 128:2 * DC * 128], w.ap()[m * 128:(m + 1) * 128, :]),
                                             dma="qw%d" % slot)
                                bank = m % 4
                                g_ = None
                                d0 = [wtok2, n_tok, bank_free[bank]]
                                idx = 0
                                for k in range(DC):
                                    for wo, mv in ((0, n2h), (0, n2l), (DC * 128, n2h)):
                                        idx += 1
                                        il = idx == 3 * DC
                                        g_ = p.do("pe", lambda e, k=k, wo=wo, mv=mv, slot=slot, bank=bank, st=(idx == 1), sp=il:
                                                  e.matmul(pmm[bank][:], wsl[slot][:, wo + k * 128:wo + (k + 1) * 128],
                                                           mv[:, k * TT:(k + 1) * TT], start=st, stop=sp),
                                                  deps=d0, inc=il)
                                        d0 = ()
                                slot_free[slot] = g_
                                gi = m % 2
                                a1 = p.do("act", lambda e, m=m, gi=gi, bank=bank:
                                          e.activation(gel[gi][:], pmm[bank][:], AF.Gelu, bias=scs["f1b"][:, m:m + 1], scale=1.0),
                                          deps=[g_, gel_free[0], gel_free[1]])
                                h_ = p.do("dve", lambda e, m=m, gi=gi: e.tensor_copy(ph[:, m * TT:(m + 1) * TT], gel[gi][:]),
                                          deps=[a1, prev_pe[0]])
                                d_ = p.do("dve", lambda e, m=m, gi=gi: e.tensor_sub(tmp[1][:], gel[gi][:], ph[:, m * TT:(m + 1) * TT]), deps=[h_])
                                p_done = p.do("dve", lambda e, m=m: e.tensor_copy(pl[:, m * TT:(m + 1) * TT], tmp[1][:]), deps=[d_])
                                gel_free[gi] = d_
                                bank_free[bank] = a1

                            last_pe = None
                            last_dve = None
                            for m in range(DC):
                                slot = m % 2
                                wtok1 = p.do("sp", lambda e, m=m, slot=slot, w=wh["f2"]:
                                             e.dma_start(wsl[slot][:, 0:HC * 128], w.ap()[m * 128:(m + 1) * 128, :]),
                                             deps=[slot_free[slot]], dma="qw%d" % slot)
                                wtok2 = p.do("sp", lambda e, m=m, slot=slot, w=wl["f2"]:
                                             e.dma_start(wsl[slot][:, HC * 128:2 * HC * 128], w.ap()[m * 128:(m + 1) * 128, :]),
                                             dma="qw%d" % slot)
                                bank = m % 4
                                g_ = None
                                d0 = [wtok2, p_done, bank_free[bank]]
                                idx = 0
                                for k in range(HC):
                                    for wo, mv in ((0, ph), (0, pl), (HC * 128, ph)):
                                        idx += 1
                                        il = idx == 3 * HC
                                        g_ = p.do("pe", lambda e, k=k, wo=wo, mv=mv, slot=slot, bank=bank, st=(idx == 1), sp=il:
                                                  e.matmul(pmm[bank][:], wsl[slot][:, wo + k * 128:wo + (k + 1) * 128],
                                                           mv[:, k * TT:(k + 1) * TT], start=st, stop=sp),
                                                  deps=d0, inc=il)
                                        d0 = ()
                                slot_free[slot] = g_
                                xi = m % 2
                                xm_tok = p.do("sp", lambda e, m=m, tau=tau, xi=xi:
                                              e.dma_start(xs2[xi][:], src.ap()[m * 128:(m + 1) * 128, tau * TT:(tau + 1) * TT]),
                                              deps=[xs_free[xi]], dma=True)
                                s1_ = p.do("dve", lambda e, m=m, tau=tau, bank=bank:
                                           e.scalar_tensor_tensor(tmp[0][:], pmm[bank][:], scs["f2b"][:, m:m + 1],
                                                                  numb[:, m * TCORE + tau * TT: m * TCORE + (tau + 1) * TT],
                                                                  ALU.add, ALU.add),
                                           deps=[g_])
                                s2_ = p.do("dve", lambda e, m=m, xi=xi: e.tensor_tensor(xo[xi][:], xs2[xi][:],
                                                                                        tmp[0][:], ALU.add),
                                           deps=[s1_, xo_free[0], xo_free[1], xm_tok])
                                st_ = p.do("act", lambda e, m=m, tau=tau, xi=xi:
                                           e.dma_start(dst.ap()[m * 128:(m + 1) * 128, tau * TT:(tau + 1) * TT], xo[xi][:]),
                                           deps=[s2_], dma=True)
                                xo_free[xi] = st_
                                xs_free[xi] = s2_
                                bank_free[bank] = s1_
                                last_pe = g_
                                last_dve = s2_
                            prev_pe[0] = last_pe
                            prev_dve[0] = last_dve
                phase_b2b()

        has_B = "B" in stages
        has_C = "C" in stages
        if "A" in stages:
            dstA = xres if (has_B or has_C) else y_out
            emit_ffn(x_in, dstA, "fA_")
            p.barrier()
        if has_B:
            srcB = xres if "A" in stages else x_in
            dstB = xres if has_C else y_out
            emit_attn(srcB, dstB)
            p.barrier()
        if has_C:
            srcC = xres if ("A" in stages or has_B) else x_in
            emit_ffn(srcC, y_out, "fC_")
            p.barrier()

        block = top.enter_context(nc.Block())
        p.replay(block, sems)
    return nc



def tile_w(w):
    out_d, in_d = w.shape
    nm, nk = out_d // 128, in_d // 128
    wt = np.ascontiguousarray(np.asarray(w).T)
    a = wt.reshape(nk, 128, nm, 128)
    a = a.transpose(2, 1, 0, 3)
    return np.ascontiguousarray(a.reshape(nm * 128, nk * 128), dtype=np.float32)


def tile_qk(w):
    wt = np.ascontiguousarray(np.asarray(w).T)
    a = wt.reshape(8, 128, 64).transpose(1, 0, 2)
    return np.ascontiguousarray(a.reshape(128, 8 * 64), dtype=np.float32)


def sc_t(v):
    v = np.asarray(v)
    n = v.shape[0] // 128
    return np.ascontiguousarray(v.reshape(n, 128).T, dtype=np.float32)


def host_inputs(ins):
    g = {k: np.asarray(v) for k, v in ins.items()}
    shared = {
        "w1f": tile_w(g["L0_ffn_w1_w"]), "w2f": tile_w(g["L0_ffn_w2_w"]),
        "h1f": tile_w(g["L0_ffn_h1_w"]), "h2f": tile_w(g["L0_ffn_h2_w"]),
        "w3f": tile_w(g["L0_ffn_w3_w"]),
        "vwf": tile_w(g["L0_attn_v_w"]), "owf": tile_w(g["L0_attn_o_w"]),
        "f1f": tile_w(g["L0_attn_f1_w"]), "f2f": tile_w(g["L0_attn_f2_w"]),
        "qwf": tile_qk(g["L0_attn_q_w"]), "kwf": tile_qk(g["L0_attn_k_w"]),
        "s1_t": sc_t(g["L0_ffn_w1_s"]), "s2_t": sc_t(g["L0_ffn_w2_s"]),
        "sh1_t": sc_t(g["L0_ffn_h1_s"]), "sh2_t": sc_t(g["L0_ffn_h2_s"]),
        "s3_t": sc_t(g["L0_ffn_w3_s"]),
        "f1b_t": sc_t(g["L0_attn_f1_b"]),
        "vb_t": sc_t(g["L0_attn_v_b"]), "ob_t": sc_t(g["L0_attn_o_b"]),
        "f2b_t": sc_t(g["L0_attn_f2_b"]),
        "ln1w_t": sc_t(g["L0_attn_ln1_w"]), "ln1b_t": sc_t(g["L0_attn_ln1_b"]),
        "ln2w_t": sc_t(g["L0_attn_ln2_w"]), "ln2b_t": sc_t(g["L0_attn_ln2_b"]),
        "qb_t": np.ascontiguousarray(g["L0_attn_q_b"].reshape(64, 1), dtype=np.float32),
        "kb_t": np.ascontiguousarray(g["L0_attn_k_b"].reshape(64, 1), dtype=np.float32),
        "ident": np.eye(128, dtype=ml_dtypes.bfloat16),
        "maskT": np.triu(np.ones((128, 128))).astype(ml_dtypes.bfloat16),
        "ones32": np.ones((128, 128), dtype=np.float32),
        "onesbf": np.ones((128, 128), dtype=ml_dtypes.bfloat16),
    }
    x = np.asarray(g["L0_attn_x"])
    in_maps = []
    for core in range(8):
        b, h = core // 2, core % 2
        xs = np.ascontiguousarray(x[b, h * TCORE:(h + 1) * TCORE, :].T, dtype=np.float32)
        m = dict(shared)
        m["x"] = xs
        m["smask"] = np.full((64, 1), 1.0 if h == 0 else 0.0, np.float32)
        m["rmask"] = np.full((64, 1), 1.0 if h == 1 else 0.0, np.float32)
        in_maps.append(m)
    return in_maps


def gather(results):
    y = np.zeros((4, 4096, 1024), np.float32)
    for core in range(8):
        b, h = core // 2, core % 2
        y[b, h * TCORE:(h + 1) * TCORE, :] = results[core]["y"].T
    return y


_NC_CACHE = [None]


def kernel(**inputs):
    in_maps = host_inputs(inputs)
    if _NC_CACHE[0] is None:
        _NC_CACHE[0] = build()
    res = run_bass_kernel_spmd(_NC_CACHE[0], in_maps, core_ids=list(range(8)))
    return gather(res.results)


# version 02 (final)
# speedup vs baseline: 1.1072x; 1.1072x over previous
import sys
from contextlib import ExitStack

sys.path.insert(0, "/opt/trn_rl_repo")
import numpy as np
import ml_dtypes
import concourse.bass as bass
import concourse.mybir as mybir
from concourse.bass_utils import run_bass_kernel_spmd

F32 = mybir.dt.float32
BF16 = mybir.dt.bfloat16
AF = mybir.ActivationFunctionType
ALU = mybir.AluOpType

D = 1024
HID = 4096
FQK = 64
TCORE = 2048
TT = 512
NTILE = TCORE // TT
CH = 128
NCHT = TT // CH
NCH = TCORE // CH
DC = D // 128
HC = HID // 128
LN_EPS = 1e-5
ATTN_EPS = 1e-6

ENG = ("pe", "act", "dve", "gps", "sp")
SECT = dict(pe="tensor", act="scalar", dve="vector", gps="gpsimd", sp="sync")
QSEM = dict(sp="qsp", act="qact", gps="qgps")
SEMS = ("pe", "act", "dve", "gps", "qsp", "qc", "qw0", "qw1", "qact", "qgps", "cc")


class P:
    def __init__(self):
        self.ops = {e: [] for e in ENG}
        self.cnt = {s: 0 for s in SEMS}
        self.waited = {e: {s: 0 for s in SEMS} for e in ENG}
        self.last = {e: None for e in ENG}
        self.maxthresh = {s: 0 for s in SEMS}
        self.label = "init" 

    def _wait(self, eng, tok):
        if tok is None:
            return
        sem, val = tok
        if val > self.maxthresh[sem]:
            self.maxthresh[sem] = val
        if val <= self.waited[eng][sem]:
            return
        self.waited[eng][sem] = val
        self.ops[eng].append(("w", sem, val, self.label))

    def do(self, eng, fn, deps=(), inc=True, dma=False, sem=None):
        if not dma and eng in ("dve", "act", "gps"):
            self._wait(eng, self.last[eng])
        if dma:
            qs = QSEM[eng] if dma is True else dma
            self._wait(eng, (qs, self.maxthresh[qs]))
        for d in deps:
            self._wait(eng, d)
        if dma:
            s, n = qs, 16
        elif sem is not None:
            s, n = sem, 1
        elif inc:
            s, n = eng, 1
        else:
            s, n = None, 0
        tok = None
        if s is not None:
            self.cnt[s] += n
            tok = (s, self.cnt[s])
            if s == eng:
                self.last[eng] = tok
        self.ops[eng].append(("x", fn, s, n, self.label))
        return tok

    def barrier(self):
        snap = [(s, v) for s, v in self.cnt.items() if v > 0]
        for e in ENG:
            for s, v in snap:
                self._wait(e, (s, v))

    def replay(self, block, sems, nc=None):
        nc_for_scope = [nc]
        for eng in ENG:
            section = getattr(block, SECT[eng])
            lst = self.ops[eng]

            def body(e, lst=lst):
                import itertools
                for label, grp in itertools.groupby(lst, key=lambda it: it[-1]):
                    with nc_for_scope[0].named_scope(label):
                        for item in grp:
                            if item[0] == "w":
                                e.wait_ge(sems[item[1]], item[2])
                            else:
                                ins = item[1](e)
                                if item[2] is not None:
                                    ins.then_inc(sems[item[2]], item[3])

            section(body)


def build(stages=("P", "A", "B", "C")):
    nc = bass.Bass("TRN2", target_bir_lowering=False, debug=False)

    def din(name, shape, dt=F32):
        return nc.dram_tensor(name, shape, dt, kind="ExternalInput")

    def dint(name, shape, dt=F32):
        return nc.dram_tensor(name, shape, dt, kind="Internal")

    x_in = din("x", [D, TCORE])
    wf = {}
    for name, nm, nk in [("w1", HC, DC), ("w2", HC, DC), ("h1", HC, HC),
                         ("h2", HC, HC), ("w3", DC, HC),
                         ("vw", DC, DC), ("ow", DC, DC),
                         ("f1", HC, DC), ("f2", DC, HC)]:
        wf[name] = din(name + "f", [nm * 128, nk * 128])
    wf["qw"] = din("qwf", [128, DC * FQK])
    wf["kw"] = din("kwf", [128, DC * FQK])
    scd = {}
    for name, w in [("s1", HC), ("s2", HC), ("sh1", HC), ("sh2", HC), ("f1b", HC),
                    ("s3", DC), ("vb", DC), ("ob", DC), ("f2b", DC),
                    ("ln1w", DC), ("ln1b", DC), ("ln2w", DC), ("ln2b", DC)]:
        scd[name] = din(name + "_t", [128, w])
    scd["qb"] = din("qb_t", [FQK, 1])
    scd["kb"] = din("kb_t", [FQK, 1])
    ident_d = din("ident", [128, 128], BF16)
    maskT_d = din("maskT", [128, 128], BF16)
    ones32_d = din("ones32", [128, 128])
    onesbf_d = din("onesbf", [128, 128], BF16)
    smask_d = din("smask", [FQK, 1])
    rmask_d = din("rmask", [FQK, 1])

    y_out = nc.dram_tensor("y", [D, TCORE], F32, kind="ExternalOutput")

    xres = dint("xres", [D, TCORE])
    wh = {k: dint(k + "hh", list(wf[k].shape), BF16) for k in wf}
    wl = {k: dint(k + "ll", list(wf[k].shape), BF16)
          for k in ("vw", "ow", "f1", "f2", "qw", "kw")}
    cc_in = dint("cc_in", [FQK, 1032])
    cc_out = dint("cc_out", [FQK, 1032])

    p = P()

    with ExitStack() as top:
        sems = {s: top.enter_context(nc.semaphore("s_" + s)) for s in SEMS}
        SBt = lambda n, s, d=F32: top.enter_context(nc.sbuf_tensor(n, s, d))

        ident = SBt("c_ident", [128, 128], BF16)
        maskT = SBt("c_maskT", [128, 128], BF16)
        ones32 = SBt("c_ones32", [128, 128])
        onesbf = SBt("c_onesbf", [128, 128], BF16)
        scs = {k: SBt("c_" + k, list(v.shape)) for k, v in scd.items()}
        smask = SBt("c_smask", [FQK, 1])
        rmask = SBt("c_rmask", [FQK, 1])
        s3h = SBt("c_s3h", [128, DC])
        qbn = SBt("c_qbn", [FQK, 1])
        kbn = SBt("c_kbn", [FQK, 1])

        for sb_t, d_t in ([(ident, ident_d), (maskT, maskT_d), (ones32, ones32_d),
                           (onesbf, onesbf_d), (smask, smask_d), (rmask, rmask_d)]
                          + [(scs[k], scd[k]) for k in scs]):
            cload = p.do("sp", lambda e, a=sb_t, b=d_t: e.dma_start(a[:], b.ap()), dma="qc")
        p.do("dve", lambda e: e.tensor_scalar_mul(s3h[:], scs["s3"][:], 0.5), deps=[cload])
        p.do("dve", lambda e: e.tensor_scalar_mul(qbn[:], scs["qb"][:], -1.0), deps=[cload])
        const_tok = p.do("dve", lambda e: e.tensor_scalar_mul(kbn[:], scs["kb"][:], -1.0), deps=[cload])

        wready = {}
        if "P" in stages:
            p.label = "prolog"
            pst32 = [SBt(f"p_st{i}", [128, 256]) for i in range(2)]
            phib = [SBt(f"p_hi{i}", [128, 256], BF16) for i in range(2)]
            ptmpb = [SBt(f"p_tm{i}", [128, 256]) for i in range(2)]
            for name in ("w1", "w2", "h1", "h2", "w3"):
                rows, cols = wf[name].shape
                rb = max(128, (4 * 1024 * 1024) // (4 * cols))
                tok = None
                for r in range(0, rows, rb):
                    r2 = min(rows, r + rb)
                    tok = p.do("gps", lambda e, o=wh[name], i=wf[name], a=r, b=r2:
                               e.dma_start(o.ap()[a:b, :], i.ap()[a:b, :]), dma=True)
                p._wait("gps", tok)
                wready[name] = tok
            par_a = [None, None]
            par_g = [None, None]
            par_d = [None, None]
            for name in ("vw", "ow", "f1", "f2", "qw", "kw"):
                rows, cols = wf[name].shape
                cb = min(cols, 256)
                blk = 0
                sl = None
                for r in range(0, rows, 128):
                    for c0 in range(0, cols, cb):
                        c1 = min(cols, c0 + cb)
                        i = blk % 2
                        blk += 1
                        n = c1 - c0
                        ld = p.do("sp", lambda e, o=pst32[i], ii=wf[name], r=r, a=c0, b=c1, n=n:
                                  e.dma_start(o[:, 0:n], ii.ap()[r:r + 128, a:b]),
                                  deps=[par_d[0], par_d[1]], dma=True)
                        h1_ = p.do("dve", lambda e, o=phib[i], ii=pst32[i], n=n:
                                   e.tensor_copy(o[:, 0:n], ii[:, 0:n]),
                                   deps=[ld, par_a[0], par_a[1]])
                        d1 = p.do("dve", lambda e, o=ptmpb[i], ii=pst32[i], h=phib[i], n=n:
                                  e.tensor_sub(o[:, 0:n], ii[:, 0:n], h[:, 0:n]),
                                  deps=[h1_, par_g[0], par_g[1]])
                        sh = p.do("act", lambda e, o=wh[name], ii=phib[i], r=r, a=c0, b=c1, n=n:
                                  e.dma_start(o.ap()[r:r + 128, a:b], ii[:, 0:n]),
                                  deps=[h1_], dma=True)
                        sl = p.do("gps", lambda e, o=wl[name], ii=ptmpb[i], r=r, a=c0, b=c1, n=n:
                                  e.dma_start(o.ap()[r:r + 128, a:b], ii[:, 0:n]),
                                  deps=[d1], dma=True)
                        par_a[i] = sh
                        par_g[i] = sl
                        par_d[i] = d1
                wready[name] = sl


        def emit_ln_stats(bufs, xsrc_fn, ps, deps_extra, stats_read):
            sq, tmp, mu, ex2, veps, sqb, y0, rb_ = bufs
            sq_free = [None, None]
            mm1 = mm2 = None
            for c in range(DC):
                i = c % 2
                stok = p.do("act", lambda e, c=c, i=i:
                            e.activation(sq[i][:], xsrc_fn(c), AF.Square, bias=0.0, scale=1.0),
                            deps=[sq_free[0], sq_free[1]] + deps_extra(c))
                mm1 = p.do("pe", lambda e, c=c:
                           e.matmul(ps[:, 0:TT], ones32[:], xsrc_fn(c), start=(c == 0), stop=(c == DC - 1)),
                           deps=[stats_read] + deps_extra(c), inc=True)
                mm2 = p.do("pe", lambda e, c=c, i=i:
                           e.matmul(ps[:, TT:2 * TT], ones32[:], sq[i][:], start=(c == 0), stop=(c == DC - 1)),
                           deps=[stok], inc=True)
                sq_free[i] = mm2
            t_mu = p.do("dve", lambda e: e.tensor_scalar_mul(mu[:], ps[:, 0:TT], 1.0 / D), deps=[mm1])
            t_e2 = p.do("dve", lambda e: e.tensor_scalar_mul(ex2[:], ps[:, TT:2 * TT], 1.0 / D), deps=[mm2])
            p.do("dve", lambda e: e.tensor_tensor(tmp[0][:], mu[:], mu[:], ALU.mult), deps=[t_mu])
            t_ve = p.do("dve", lambda e: e.scalar_tensor_tensor(veps[:], ex2[:], LN_EPS, tmp[0][:],
                                                                ALU.add, ALU.subtract), deps=[t_e2])
            t_s = p.do("act", lambda e: e.activation(sqb[:], veps[:], AF.Sqrt, bias=0.0, scale=1.0), deps=[t_ve])
            t_y0 = p.do("dve", lambda e: e.reciprocal(y0[:], sqb[:]), deps=[t_s])
            p.do("dve", lambda e: e.tensor_tensor(tmp[0][:], y0[:], y0[:], ALU.mult), deps=[t_y0])
            p.do("dve", lambda e: e.tensor_tensor(tmp[1][:], veps[:], tmp[0][:], ALU.mult))
            p.do("dve", lambda e: e.tensor_scalar(tmp[0][:], tmp[1][:], -0.5, 1.5, ALU.mult, ALU.add))
            t_r = p.do("dve", lambda e: e.tensor_tensor(rb_[:], y0[:], tmp[0][:], ALU.mult))
            return t_mu, t_r, t_ve, mm2

        def emit_ffn(src, dst, tag):
            p.label = tag
            with ExitStack() as fs:
                fb = lambda n, s, d=F32: fs.enter_context(nc.sbuf_tensor(tag + n, s, d))
                xt = fb("xt", [128, DC * TT])
                th = fb("th", [128, DC * TT], BF16)
                tl = fb("tl", [128, DC * TT], BF16)
                hh = fb("hh", [128, HC * TT], BF16)
                hl = fb("hl", [128, HC * TT], BF16)
                gh = fb("gh", [128, HC * TT], BF16)
                gl = fb("gl", [128, HC * TT], BF16)
                sq = [fb("sq0", [128, TT])]
                sq.append(sq[0])
                tmp = [fb(f"tm{i}", [128, TT]) for i in range(2)]
                mu = fb("mu", [128, TT]); ex2 = fb("ex2", [128, TT])
                veps = fb("veps", [128, TT])
                sqb = ex2; y0 = fb("y0", [128, TT])
                rb_ = fb("rb", [128, TT])
                silu = [fb(f"si{i}", [128, TT]) for i in range(2)]
                gel = silu
                xo = [fb("xo0", [128, TT])]
                xo.append(xo[0])
                xs2 = silu
                wsl = [fb(f"ws{i}", [128, 4096], BF16) for i in range(2)]
                pst = fs.enter_context(nc.psum_tensor(tag + "pst", [128, 1024], F32))
                pmm = [fs.enter_context(nc.psum_tensor(f"{tag}pm{i}", [128, TT], F32)) for i in range(4)]
                lnbufs = (sq, tmp, mu, ex2, veps, sqb, y0, rb_)

                prev_pe = [None]
                stats_read = [None]
                xt_free = [None]
                th_free = [None]
                xo_free = [None, None]
                xs_free = [None, None]
                slot_free = [None, None]
                bank_free = [None] * 4
                silu_free = [None, None]
                gel_free = [None, None]
                t_toks = [None] * NTILE

                def group(bank, slot, passes, nk, deps):
                    g_ = None
                    d0 = list(deps)
                    total = nk * len(passes)
                    idx = 0
                    for k in range(nk):
                        for (wo, mv) in passes:
                            idx += 1
                            g_ = p.do("pe", lambda e, k=k, wo=wo, mv=mv, slot=slot, bank=bank,
                                      st=(idx == 1), sp=(idx == total):
                                      e.matmul(pmm[bank][:], wsl[slot][:, wo + k * 128:wo + (k + 1) * 128],
                                               mv[:, k * TT:(k + 1) * TT], start=st, stop=sp),
                                      deps=d0, inc=(idx == total))
                            d0 = ()
                    return g_

                def ln_tile(tau):
                    xtok = []
                    for c in range(DC):
                        tok = p.do("sp", lambda e, c=c, tau=tau:
                                   e.dma_start(xt[:, c * TT:(c + 1) * TT],
                                               src.ap()[c * 128:(c + 1) * 128, tau * TT:(tau + 1) * TT]),
                                   deps=[xt_free[0]], dma=True)
                        xtok.append(tok)
                    t_mu, t_r, t_v, _ = emit_ln_stats(
                        lnbufs, lambda c: xt[:, c * TT:(c + 1) * TT], pst,
                        lambda c: [xtok[-1]], stats_read[0])
                    stats_read[0] = t_v
                    t_tok = None
                    for c in range(DC):
                        a = p.do("dve", lambda e, c=c: e.tensor_tensor(tmp[0][:], xt[:, c * TT:(c + 1) * TT], mu[:], ALU.subtract),
                                 deps=[t_mu, t_r, xtok[-1]])
                        b = p.do("dve", lambda e: e.tensor_tensor(tmp[1][:], tmp[0][:], rb_[:], ALU.mult), deps=[a])
                        h_ = p.do("dve", lambda e, c=c: e.tensor_copy(th[:, c * TT:(c + 1) * TT], tmp[1][:]),
                                  deps=[b, th_free[0]])
                        d_ = p.do("dve", lambda e, c=c: e.tensor_sub(tmp[0][:], tmp[1][:], th[:, c * TT:(c + 1) * TT]), deps=[h_])
                        t_tok = p.do("dve", lambda e, c=c: e.tensor_copy(tl[:, c * TT:(c + 1) * TT], tmp[0][:]), deps=[d_])
                    xt_free[0] = t_tok
                    return t_tok

                t_toks[0] = ln_tile(0)
                for tau in range(NTILE):
                    t_tok = t_toks[tau]
                    h_done = None
                    for m in range(HC):
                        slot = m % 2
                        p.do("sp", lambda e, m=m, slot=slot:
                             e.dma_start(wsl[slot][:, 0:DC * 128], wh["w1"].ap()[m * 128:(m + 1) * 128, :]),
                             deps=[slot_free[slot], wready.get("w1")], dma="qw%d" % slot)
                        wtok = p.do("sp", lambda e, m=m, slot=slot:
                                    e.dma_start(wsl[slot][:, DC * 128:2 * DC * 128], wh["w2"].ap()[m * 128:(m + 1) * 128, :]),
                                    deps=[wready.get("w2")], dma="qw%d" % slot)
                        bA, bB = (2 * m) % 4, (2 * m + 1) % 4
                        gA = group(bA, slot, [(0, th), (0, tl)], DC, [wtok, t_tok, bank_free[bA]])
                        gB = group(bB, slot, [(DC * 128, th), (DC * 128, tl)], DC, [bank_free[bB]])
                        slot_free[slot] = gB
                        si = m % 2
                        a1 = p.do("act", lambda e, m=m, si=si, bA=bA:
                                  e.activation(silu[si][:], pmm[bA][:], AF.Silu, bias=0.0,
                                               scale=scs["s1"][:, m:m + 1]),
                                  deps=[gA, silu_free[si]])
                        d1 = p.do("dve", lambda e, m=m, si=si, bB=bB:
                                  e.scalar_tensor_tensor(tmp[0][:], pmm[bB][:], scs["s2"][:, m:m + 1],
                                                         silu[si][:], ALU.mult, ALU.mult),
                                  deps=[gB, a1])
                        silu_free[si] = d1
                        h_ = p.do("dve", lambda e, m=m: e.tensor_copy(hh[:, m * TT:(m + 1) * TT], tmp[0][:]), deps=[d1])
                        d2 = p.do("dve", lambda e, m=m: e.tensor_sub(tmp[1][:], tmp[0][:], hh[:, m * TT:(m + 1) * TT]), deps=[h_])
                        h_done = p.do("dve", lambda e, m=m: e.tensor_copy(hl[:, m * TT:(m + 1) * TT], tmp[1][:]), deps=[d2])
                        bank_free[bA] = a1
                        bank_free[bB] = d1
                    th_free[0] = slot_free[(HC - 1) % 2]

                    if tau + 1 < NTILE:
                        t_toks[tau + 1] = ln_tile(tau + 1)

                    def hidden(wname, sname, src_h, src_l, dst_h, dst_l, ready):
                        done = ready
                        for m in range(HC):
                            slot = m % 2
                            wtok = p.do("sp", lambda e, m=m, slot=slot, w=wh[wname]:
                                        e.dma_start(wsl[slot][:, 0:HC * 128], w.ap()[m * 128:(m + 1) * 128, :]),
                                        deps=[slot_free[slot], wready.get(wname)], dma="qw%d" % slot)
                            bank = m % 4
                            g_ = group(bank, slot, [(0, src_h), (0, src_l)], HC,
                                       [wtok, ready, bank_free[bank]])
                            slot_free[slot] = g_
                            gi = m % 2
                            a1 = p.do("act", lambda e, m=m, gi=gi, bank=bank, sname=sname:
                                      e.activation(gel[gi][:], pmm[bank][:], AF.Gelu, bias=0.0,
                                                   scale=scs[sname][:, m:m + 1]),
                                      deps=[g_, gel_free[gi]])
                            h_ = p.do("dve", lambda e, m=m, gi=gi, dh=dst_h:
                                      e.tensor_copy(dh[:, m * TT:(m + 1) * TT], gel[gi][:]), deps=[a1])
                            d_ = p.do("dve", lambda e, m=m, gi=gi, dh=dst_h:
                                      e.tensor_sub(tmp[1][:], gel[gi][:], dh[:, m * TT:(m + 1) * TT]), deps=[h_])
                            done = p.do("dve", lambda e, m=m, dl=dst_l:
                                        e.tensor_copy(dl[:, m * TT:(m + 1) * TT], tmp[1][:]), deps=[d_])
                            gel_free[gi] = d_
                            bank_free[bank] = a1
                        return done

                    g_done = hidden("h1", "sh1", hh, hl, gh, gl, h_done)
                    f_done = hidden("h2", "sh2", gh, gl, hh, hl, g_done)

                    last_pe = None
                    for m in range(DC):
                        slot = m % 2
                        wtok = p.do("sp", lambda e, m=m, slot=slot:
                                    e.dma_start(wsl[slot][:, 0:HC * 128], wh["w3"].ap()[m * 128:(m + 1) * 128, :]),
                                    deps=[slot_free[slot], wready.get("w3")], dma="qw%d" % slot)
                        bank = m % 4
                        g_ = group(bank, slot, [(0, hh), (0, hl)], HC, [wtok, f_done, bank_free[bank]])
                        slot_free[slot] = g_
                        xi = m % 2
                        xm_tok = p.do("sp", lambda e, m=m, tau=tau, xi=xi:
                                      e.dma_start(xs2[xi][:], src.ap()[m * 128:(m + 1) * 128, tau * TT:(tau + 1) * TT]),
                                      deps=[xs_free[xi], gel_free[0], gel_free[1]], dma=True)
                        s_ = p.do("dve", lambda e, m=m, xi=xi, bank=bank:
                                  e.scalar_tensor_tensor(xo[xi][:], pmm[bank][:], s3h[:, m:m + 1],
                                                         xs2[xi][:], ALU.mult, ALU.add),
                                  deps=[g_, xm_tok, xo_free[0], xo_free[1], const_tok])
                        st_ = p.do("act", lambda e, m=m, tau=tau, xi=xi:
                                   e.dma_start(dst.ap()[m * 128:(m + 1) * 128, tau * TT:(tau + 1) * TT], xo[xi][:]),
                                   deps=[s_], dma=True)
                        xo_free[xi] = st_
                        xs_free[xi] = s_
                        bank_free[bank] = s_
                        last_pe = g_
                    silu_free[0] = silu_free[1] = xs_free[1] if xs_free[1] else xs_free[0]
                    gel_free[0] = gel_free[1] = silu_free[0]
                    prev_pe[0] = last_pe


        def emit_attn(src, dst):
            with ExitStack() as bso:
                bb = lambda n, s, d=F32: bso.enter_context(nc.sbuf_tensor("b_" + n, s, d))
                numb = bb("num", [128, DC * TCORE])
                mid = bso.enter_context(ExitStack())
                bm = lambda n, s, d=F32: mid.enter_context(nc.sbuf_tensor("b_" + n, s, d))
                qhi = bm("qhi", [FQK, TCORE], BF16)
                qlo = bm("qlo", [FQK, TCORE], BF16)
                mu_all = bm("mua", [128, TCORE])
                r0_all = bm("r0a", [128, TCORE])
                den_all = bm("dena", [128, TCORE])
                kvr = bm("kvr", [FQK, 1032])
                kvrh = bm("kvrh", [FQK, 1024], BF16)
                kvrl = bm("kvrl", [FQK, 1024], BF16)
                ksrh = bm("ksrh", [FQK, 128], BF16)
                ksrl = bm("ksrl", [FQK, 128], BF16)
                kstmp = bm("kstmp", [FQK, 256])

                def phase_b1():
                    p.label = "b1"
                    with ExitStack() as b1:
                        b1b = lambda n, s, d=F32: b1.enter_context(nc.sbuf_tensor("b1_" + n, s, d))
                        xt = b1b("xt", [128, DC * TT])
                        nh = b1b("nh", [128, DC * TT], BF16)
                        nl = b1b("nl", [128, DC * TT], BF16)
                        vh = b1b("vh", [128, DC * TT], BF16)
                        vl = b1b("vl", [128, DC * TT], BF16)
                        kh = b1b("kh", [FQK, TT], BF16)
                        kl = b1b("kl", [FQK, TT], BF16)
                        qf = b1b("qf", [FQK, TT])
                        kf = qf
                        sq = [b1b("sq0", [128, TT])]
                        sq.append(sq[0])
                        tmp = [b1b(f"tm{i}", [128, TT]) for i in range(2)]
                        mu = b1b("mu", [128, TT]); ex2 = b1b("ex2", [128, TT])
                        veps = b1b("veps", [128, TT])
                        sqb = ex2; y0 = b1b("y0", [128, TT])
                        rb_ = b1b("rb", [128, TT]); rc_ = b1b("rc", [128, TT])
                        aeps = veps
                        vth = b1b("vth", [128, 1024], BF16)
                        vtl = b1b("vtl", [128, 1024], BF16)
                        kts = b1b("kts", [128, 128], BF16)
                        smf = b1b("smf", [128, 128])
                        smh = b1b("smh", [128, 128], BF16)
                        sml = b1b("sml", [128, 128], BF16)
                        kvf = [b1b(f"kvf{i}", [FQK, 1032]) for i in range(2)]
                        kvbh = b1b("kvbh", [FQK, 1024], BF16)
                        kvbl = b1b("kvbl", [FQK, 1024], BF16)
                        ksr = [b1b(f"ksr{i}", [FQK, 128]) for i in range(2)]
                        ksi = b1b("ksi", [FQK, 1])
                        krh = b1b("krh", [FQK, 128], BF16)
                        krl = b1b("krl", [FQK, 128], BF16)
                        wsl = [b1b("ws0", [128, 1024], BF16), b1b("ws1", [128, 2048], BF16)]
                        pnum = b1.enter_context(nc.psum_tensor("b1_pnum", [128, 1024], F32))
                        pkv = b1.enter_context(nc.psum_tensor("b1_pkv", [FQK, 1024], F32))
                        pvth = b1.enter_context(nc.psum_tensor("b1_pvth", [128, 1024], BF16))
                        pvtl = b1.enter_context(nc.psum_tensor("b1_pvtl", [128, 1024], BF16))
                        pmsf = b1.enter_context(nc.psum_tensor("b1_pmsf", [128, 256], F32))
                        pmsb = b1.enter_context(nc.psum_tensor("b1_pmsb", [128, 128], BF16))
                        lnbufs = (sq, tmp, mu, ex2, veps, sqb, y0, rb_)

                        prev_pe = [None]; prev_dve = [None]; prev_act = [None]
                        stats_read = [None]
                        slot_free = [None, None]
                        pnum_free = [None]
                        vt_rd = [None, None]
                        vt_ps = [None, None]
                        kt_rd = [None]; kt_ps = [None]
                        sm_rd = [None]
                        sp_ps = [None]
                        den_ps = [None]
                        kv_evac = [None]
                        kv_tok = [None]
                        ksr_tok = [None, None]
                        kr_free = [None]
                        z = p.do("dve", lambda e: e.memset(ksr[1][:], 0.0))
                        ksr_tok[1] = z
                        p.do("dve", lambda e: e.memset(kvf[1][:], 0.0))
                        p.do("dve", lambda e: e.memset(kvf[0][:], 0.0))
                        gnum = gkv = gden = ks2 = ekt = ne = None

                        for tau in range(NTILE):
                            xtok = []
                            for c in range(DC):
                                tok = p.do("sp", lambda e, c=c, tau=tau:
                                           e.dma_start(xt[:, c * TT:(c + 1) * TT],
                                                       src.ap()[c * 128:(c + 1) * 128, tau * TT:(tau + 1) * TT]),
                                           deps=[prev_pe[0], prev_dve[0], prev_act[0]], dma=True)
                                xtok.append(tok)
                            t_mu, t_r, t_v, _ = emit_ln_stats(
                                lnbufs, lambda c: xt[:, c * TT:(c + 1) * TT], pnum,
                                lambda c: [xtok[-1], pnum_free[0]], stats_read[0])
                            stats_read[0] = t_v
                            p.do("dve", lambda e, tau=tau: e.tensor_copy(mu_all[:, tau * TT:(tau + 1) * TT], mu[:]), deps=[t_mu])
                            t_r0s = p.do("dve", lambda e, tau=tau: e.tensor_copy(r0_all[:, tau * TT:(tau + 1) * TT], rb_[:]), deps=[t_r])
                            p.do("dve", lambda e: e.tensor_tensor(tmp[0][:], rb_[:], rb_[:], ALU.mult), deps=[t_r])
                            p.do("dve", lambda e: e.scalar_tensor_tensor(tmp[1][:], veps[:], -LN_EPS, tmp[0][:],
                                                                         ALU.add, ALU.mult))
                            c3 = p.do("dve", lambda e: e.tensor_scalar(aeps[:], tmp[1][:], 1.0, LN_EPS, ALU.mult, ALU.add))
                            c4 = p.do("act", lambda e: e.activation(sqb[:], aeps[:], AF.Sqrt, bias=0.0, scale=1.0), deps=[c3])
                            c5 = p.do("dve", lambda e: e.reciprocal(y0[:], sqb[:]), deps=[c4])
                            p.do("dve", lambda e: e.tensor_tensor(tmp[0][:], y0[:], y0[:], ALU.mult), deps=[c5])
                            p.do("dve", lambda e: e.tensor_tensor(tmp[1][:], aeps[:], tmp[0][:], ALU.mult))
                            p.do("dve", lambda e: e.tensor_scalar(tmp[0][:], tmp[1][:], -0.5, 1.5, ALU.mult, ALU.add))
                            c9 = p.do("dve", lambda e: e.tensor_tensor(rc_[:], y0[:], tmp[0][:], ALU.mult))
                            t_rr = p.do("dve", lambda e: e.tensor_tensor(rc_[:], rb_[:], rc_[:], ALU.mult), deps=[c9, t_r0s])
                            del c9
                            n_tok = None
                            for c in range(DC):
                                a = p.do("dve", lambda e, c=c: e.tensor_tensor(tmp[0][:], xt[:, c * TT:(c + 1) * TT], mu[:], ALU.subtract),
                                         deps=[t_mu, t_rr, xtok[-1], prev_pe[0]])
                                b = p.do("dve", lambda e: e.tensor_tensor(tmp[1][:], tmp[0][:], rc_[:], ALU.mult), deps=[a])
                                b2 = p.do("dve", lambda e, c=c: e.tensor_scalar(tmp[0][:], tmp[1][:], scs["ln1w"][:, c:c + 1],
                                                                                scs["ln1b"][:, c:c + 1], ALU.mult, ALU.add), deps=[b])
                                h_ = p.do("dve", lambda e, c=c: e.tensor_copy(nh[:, c * TT:(c + 1) * TT], tmp[0][:]), deps=[b2])
                                d_ = p.do("dve", lambda e, c=c: e.tensor_sub(tmp[1][:], tmp[0][:], nh[:, c * TT:(c + 1) * TT]), deps=[h_])
                                n_tok = p.do("dve", lambda e, c=c: e.tensor_copy(nl[:, c * TT:(c + 1) * TT], tmp[1][:]), deps=[d_])

                            def qk_proj(wname, bias_sb, bias_neg, out_f, oh_ap, ol_ap, psl, extra):
                                wtok1 = p.do("sp", lambda e, w=wh[wname]: e.dma_start(wsl[0][:, 0:DC * FQK], w.ap()[:, :]),
                                             deps=[slot_free[0]], dma="qw0")
                                wtok2 = p.do("sp", lambda e, w=wl[wname]: e.dma_start(wsl[0][:, DC * FQK:2 * DC * FQK], w.ap()[:, :]),
                                             dma="qw0")
                                g_ = None
                                d0 = [wtok2, n_tok, pnum_free[0]] + list(extra)
                                idx = 0
                                for k in range(DC):
                                    for wo, mv in ((0, nh), (0, nl), (DC * FQK, nh)):
                                        idx += 1
                                        il = idx == 3 * DC
                                        g_ = p.do("pe", lambda e, k=k, wo=wo, mv=mv, psl=psl, st=(idx == 1), sp=il:
                                                  e.matmul(pnum[0:FQK, psl * TT:(psl + 1) * TT],
                                                           wsl[0][:, wo + k * FQK:wo + (k + 1) * FQK],
                                                           mv[:, k * TT:(k + 1) * TT], start=st, stop=sp),
                                                  deps=d0, inc=il)
                                        d0 = ()
                                slot_free[0] = g_
                                e1 = p.do("act", lambda e, psl=psl, bb_=bias_sb: e.activation(tmp[0][0:FQK, :], pnum[0:FQK, psl * TT:(psl + 1) * TT],
                                                                                              AF.Relu, bias=bb_[:], scale=1.0),
                                          deps=[g_, n_tok, prev_act[0]] + list(extra))
                                e2 = p.do("act", lambda e, psl=psl, bb_=bias_neg: e.activation(tmp[1][0:FQK, :], pnum[0:FQK, psl * TT:(psl + 1) * TT],
                                                                                               AF.Relu, bias=bb_[:], scale=-1.0),
                                          deps=[g_, n_tok, const_tok])
                                e3 = p.do("act", lambda e: e.activation(tmp[1][0:FQK, :], tmp[1][0:FQK, :], AF.Exp, bias=0.0, scale=-1.0), deps=[e2])
                                f_ = p.do("dve", lambda e, o=out_f: e.tensor_tensor(o[:], tmp[0][0:FQK, :], tmp[1][0:FQK, :], ALU.add),
                                          deps=[e1, e3, prev_pe[0]])
                                h_ = p.do("dve", lambda e, o=oh_ap, f=out_f: e.tensor_copy(o, f[:]), deps=[f_])
                                d_ = p.do("dve", lambda e, o=out_f, hq=oh_ap: e.tensor_sub(tmp[0][0:FQK, :], o[:], hq), deps=[h_])
                                l_ = p.do("dve", lambda e, o=ol_ap: e.tensor_copy(o, tmp[0][0:FQK, :]), deps=[d_])
                                return g_, e2, l_

                            gq, e2q, q_tok = qk_proj("qw", scs["qb"], qbn, qf,
                                                     qhi[:, tau * TT:(tau + 1) * TT],
                                                     qlo[:, tau * TT:(tau + 1) * TT], 0, [])
                            gk, e2k, k_tok = qk_proj("kw", scs["kb"], kbn, kf, kh[:], kl[:], 1, [q_tok])
                            v_tok = None
                            vg = [e2q, e2k]
                            for m in range(DC):
                                wtok1 = p.do("sp", lambda e, m=m, w=wh["vw"]:
                                             e.dma_start(wsl[1][:, 0:DC * 128], w.ap()[m * 128:(m + 1) * 128, :]),
                                             deps=[slot_free[1]], dma="qw1")
                                wtok2 = p.do("sp", lambda e, m=m, w=wl["vw"]:
                                             e.dma_start(wsl[1][:, DC * 128:2 * DC * 128], w.ap()[m * 128:(m + 1) * 128, :]),
                                             dma="qw1")
                                psl = m % 2
                                g_ = None
                                d0 = [wtok2, n_tok, vg[psl]]
                                idx = 0
                                for k in range(DC):
                                    for wo, mv in ((0, nh), (0, nl), (DC * 128, nh)):
                                        idx += 1
                                        il = idx == 3 * DC
                                        g_ = p.do("pe", lambda e, k=k, wo=wo, mv=mv, psl=psl, st=(idx == 1), sp=il:
                                                  e.matmul(pnum[:, psl * TT:(psl + 1) * TT],
                                                           wsl[1][:, wo + k * 128:wo + (k + 1) * 128],
                                                           mv[:, k * TT:(k + 1) * TT], start=st, stop=sp),
                                                  deps=d0, inc=il)
                                        d0 = ()
                                slot_free[1] = g_
                                s1_ = p.do("dve", lambda e, m=m, psl=psl:
                                           e.tensor_scalar_add(tmp[0][:], pnum[:, psl * TT:(psl + 1) * TT], scs["vb"][:, m:m + 1]),
                                           deps=[g_, prev_pe[0]])
                                h_ = p.do("dve", lambda e, m=m: e.tensor_copy(vh[:, m * TT:(m + 1) * TT], tmp[0][:]), deps=[s1_])
                                d_ = p.do("dve", lambda e, m=m: e.tensor_sub(tmp[1][:], tmp[0][:], vh[:, m * TT:(m + 1) * TT]), deps=[h_])
                                v_tok = p.do("dve", lambda e, m=m: e.tensor_copy(vl[:, m * TT:(m + 1) * TT], tmp[1][:]), deps=[d_])
                                vg[psl] = s1_
                            pnum_free[0] = v_tok

                            for cc_ in range(NCHT):
                                g = tau * NCHT + cc_
                                cur, prv = g % 2, (g + 1) % 2
                                tvh = None
                                for j in range(DC):
                                    tvh = p.do("pe", lambda e, j=j, cc_=cc_:
                                               e.transpose(pvth[:, j * 128:(j + 1) * 128],
                                                           vh[:, j * TT + cc_ * CH: j * TT + (cc_ + 1) * CH], ident[:]),
                                               deps=[v_tok, vt_ps[0]], inc=(j == DC - 1))
                                evh = p.do("act", lambda e: e.copy(vth[:], pvth[:]), deps=[tvh, vt_rd[0]])
                                vt_ps[0] = evh
                                tvl = None
                                for j in range(DC):
                                    tvl = p.do("pe", lambda e, j=j, cc_=cc_:
                                               e.transpose(pvtl[:, j * 128:(j + 1) * 128],
                                                           vl[:, j * TT + cc_ * CH: j * TT + (cc_ + 1) * CH], ident[:]),
                                               deps=[v_tok, vt_ps[1]], inc=(j == DC - 1))
                                evl = p.do("act", lambda e: e.copy(vtl[:], pvtl[:]), deps=[tvl, vt_rd[1]])
                                vt_ps[1] = evl
                                p.do("pe", lambda e, cc_=cc_: e.transpose(pmsb[0:CH, 0:FQK], kh[:, cc_ * CH:(cc_ + 1) * CH], ident[0:FQK, 0:FQK]),
                                     deps=[k_tok, kt_ps[0]], inc=False)
                                tk2 = p.do("pe", lambda e, cc_=cc_: e.transpose(pmsb[0:CH, FQK:2 * FQK], kl[:, cc_ * CH:(cc_ + 1) * CH], ident[0:FQK, 0:FQK]),
                                           inc=True)
                                ekt = p.do("act", lambda e: e.copy(kts[:], pmsb[:]), deps=[tk2, kt_rd[0]])
                                kt_ps[0] = ekt
                                p.do("pe", lambda e, cc_=cc_, g=g: e.matmul(pmsf[:, 0:128], kh[:, cc_ * CH:(cc_ + 1) * CH],
                                                                            qhi[:, g * CH:(g + 1) * CH], start=True, stop=False),
                                     deps=[q_tok, k_tok, sp_ps[0]], inc=False)
                                p.do("pe", lambda e, cc_=cc_, g=g: e.matmul(pmsf[:, 0:128], kh[:, cc_ * CH:(cc_ + 1) * CH],
                                                                            qlo[:, g * CH:(g + 1) * CH], start=False, stop=False),
                                     inc=False)
                                sp3 = p.do("pe", lambda e, cc_=cc_, g=g: e.matmul(pmsf[:, 0:128], kl[:, cc_ * CH:(cc_ + 1) * CH],
                                                                                  qhi[:, g * CH:(g + 1) * CH], start=False, stop=True),
                                           inc=True)
                                m1 = p.do("dve", lambda e: e.tensor_tensor(smf[:], pmsf[:, 0:128], maskT[:], ALU.mult),
                                          deps=[sp3, sm_rd[0]])
                                m2 = p.do("dve", lambda e: e.tensor_copy(smh[:], smf[:]), deps=[m1])
                                m3 = p.do("dve", lambda e: e.tensor_sub(smf[:], smf[:], smh[:]), deps=[m2])
                                m4 = p.do("dve", lambda e: e.tensor_copy(sml[:], smf[:]), deps=[m3])
                                sp_ps[0] = m1
                                kr1 = p.do("dve", lambda e, prv=prv: e.tensor_copy(krh[:], ksr[prv][:]),
                                           deps=[ksr_tok[prv], kr_free[0]])
                                kr2 = p.do("dve", lambda e, prv=prv: e.tensor_sub(kstmp[:, 0:128], ksr[prv][:], krh[:]), deps=[kr1])
                                kr3 = p.do("dve", lambda e: e.tensor_copy(krl[:], kstmp[:, 0:128]), deps=[kr2])
                                d0 = [m4, evh, evl, pnum_free[0], kv_tok[0], stats_read[0]]
                                nmm = 6 if g > 0 else 3
                                idx = 0
                                for j in range(DC):
                                    mml = [(vth, smh, None), (vth, sml, None), (vtl, smh, None)]
                                    if g > 0:
                                        mml += [(kvbh, qhi, g), (kvbh, qlo, g), (kvbl, qhi, g)]
                                    for t_i, (lh, rh, qg) in enumerate(mml):
                                        idx += 1
                                        il = idx == nmm * DC
                                        if qg is None:
                                            gnum = p.do("pe", lambda e, j=j, lh=lh, rh=rh, st=(t_i == 0), sp=(t_i == len(mml) - 1):
                                                        e.matmul(pnum[:, j * 128:(j + 1) * 128], lh[:, j * 128:(j + 1) * 128],
                                                                 rh[:], start=st, stop=sp),
                                                        deps=d0, inc=il)
                                        else:
                                            gnum = p.do("pe", lambda e, j=j, lh=lh, rh=rh, g=g, sp=(t_i == len(mml) - 1):
                                                        e.matmul(pnum[:, j * 128:(j + 1) * 128], lh[:, j * 128:(j + 1) * 128],
                                                                 rh[:, g * CH:(g + 1) * CH], start=False, stop=sp),
                                                        deps=d0, inc=il)
                                        d0 = ()
                                d0 = [m4, den_ps[0], kr3]
                                dml = [(onesbf, smh, None, True), (onesbf, sml, None, False)]
                                if g > 0:
                                    dml += [(krh, qhi, g, False), (krh, qlo, g, False), (krl, qhi, g, False)]
                                for t_i, (lh, rh, qg, st_) in enumerate(dml):
                                    il = t_i == len(dml) - 1
                                    if qg is None:
                                        gden = p.do("pe", lambda e, lh=lh, rh=rh, st=st_, sp=il:
                                                    e.matmul(pmsf[:, 128:256], lh[:], rh[:], start=st, stop=sp),
                                                    deps=d0, inc=il)
                                    else:
                                        gden = p.do("pe", lambda e, lh=lh, rh=rh, g=g, sp=il:
                                                    e.matmul(pmsf[:, 128:256], lh[:], rh[:, g * CH:(g + 1) * CH],
                                                             start=False, stop=sp),
                                                    deps=d0, inc=il)
                                    d0 = ()
                                kr_free[0] = gden
                                de = p.do("dve", lambda e, g=g: e.tensor_scalar(den_all[:, g * CH:(g + 1) * CH], pmsf[:, 128:256],
                                                                                1.0, ATTN_EPS, ALU.mult, ALU.add), deps=[gden])
                                den_ps[0] = de
                                ne = None
                                for j in range(DC):
                                    ne = p.do("dve", lambda e, j=j, g=g:
                                              e.tensor_copy(numb[:, j * TCORE + g * CH: j * TCORE + (g + 1) * CH],
                                                            pnum[:, j * 128:(j + 1) * 128]), deps=[gnum])
                                pnum_free[0] = ne
                                sm_rd[0] = gnum
                                vt_rd = [gnum, gnum]
                                d0 = [ekt, evh, evl, kv_evac[0]]
                                idx = 0
                                for (wo, mvv) in ((0, vth), (0, vtl), (FQK, vth)):
                                    for hf in range(2):
                                        idx += 1
                                        gkv = p.do("pe", lambda e, wo=wo, mvv=mvv, hf=hf,
                                                   st=(idx <= 2), sp=(idx >= 5):
                                                   e.matmul(pkv[:, hf * TT:(hf + 1) * TT], kts[:, wo:wo + FQK],
                                                            mvv[:, hf * TT:(hf + 1) * TT], start=st, stop=sp),
                                                   deps=d0, inc=(idx == 6))
                                        d0 = ()
                                kt_rd[0] = gkv
                                kv1 = p.do("dve", lambda e, cur=cur, prv=prv:
                                           e.tensor_tensor(kvf[cur][0:FQK, 0:1024], kvf[prv][0:FQK, 0:1024], pkv[:], ALU.add), deps=[gkv])
                                kv_evac[0] = kv1
                                kv2 = p.do("dve", lambda e, cur=cur: e.tensor_copy(kvbh[:], kvf[cur][0:FQK, 0:1024]), deps=[kv1, gnum])
                                kv3 = p.do("dve", lambda e, cur=cur, prv=prv: e.tensor_sub(kvf[prv][0:FQK, 0:1024], kvf[cur][0:FQK, 0:1024], kvbh[:]), deps=[kv2])
                                kv4 = p.do("dve", lambda e, prv=prv: e.tensor_copy(kvbl[:], kvf[prv][0:FQK, 0:1024]), deps=[kv3])
                                kv_tok[0] = kv4
                                ks1 = p.do("dve", lambda e, cc_=cc_: e.reduce_sum(ksi[:], kf[:, cc_ * CH:(cc_ + 1) * CH],
                                                                                  axis=mybir.AxisListType.X), deps=[k_tok])
                                ks2 = p.do("dve", lambda e, cur=cur, prv=prv: e.tensor_scalar_add(ksr[cur][:], ksr[prv][:], ksi[:]),
                                           deps=[ks1, ksr_tok[cur], kr3])
                                ksr_tok[cur] = ks2
                            prev_pe[0] = gkv
                            prev_dve[0] = ks2
                            prev_act[0] = ekt

                        p.label = "cc"
                        lastp = (NCH - 1) % 2
                        ccs = xt
                        pk1 = p.do("dve", lambda e, lastp=lastp: e.tensor_copy(ccs[0:FQK, 0:1024], kvf[lastp][0:FQK, 0:1024]),
                                   deps=[kv_evac[0], kv_tok[0], prev_pe[0]])
                        pk2 = p.do("dve", lambda e, lastp=lastp: e.tensor_copy(ccs[0:FQK, 1024:1025], ksr[lastp][:, 0:1]),
                                   deps=[ksr_tok[lastp]])
                        pk3 = p.do("dve", lambda e: e.memset(ccs[0:FQK, 1025:1032], 0.0))
                        oth = (NCH) % 2
                        pk4 = p.do("dve", lambda e, oth=oth: e.tensor_scalar_mul(kvf[oth][0:FQK, 0:1032], ccs[0:FQK, 0:1032], smask[:]), deps=[pk1, pk2, pk3, const_tok])
                        sd = p.do("gps", lambda e, oth=oth: e.dma_start(cc_in.ap(), kvf[oth][0:FQK, 0:1032]), deps=[pk4], dma=True)
                        cct = p.do("gps", lambda e: e.collective_compute(
                            "AllReduce", ALU.add, replica_groups=[[0, 1], [2, 3], [4, 5], [6, 7]],
                            ins=[cc_in.ap()], outs=[cc_out.ap()]), deps=[sd], sem="cc")
                        rc1 = p.do("gps", lambda e: e.dma_start(kvr[:], cc_out.ap()), deps=[cct], dma=True)
                        rm1 = p.do("dve", lambda e: e.tensor_scalar_mul(kvr[:], kvr[:], rmask[:]), deps=[rc1, const_tok])
                        rm2 = p.do("dve", lambda e: e.tensor_copy(kvrh[:], kvr[:, 0:1024]), deps=[rm1])
                        rm3 = p.do("dve", lambda e: e.tensor_sub(kvf[0][0:FQK, 0:1024], kvr[:, 0:1024], kvrh[:]), deps=[rm2])
                        rm4 = p.do("dve", lambda e: e.tensor_copy(kvrl[:], kvf[0][0:FQK, 0:1024]), deps=[rm3])
                        rk1 = p.do("dve", lambda e: e.tensor_scalar_mul(kstmp[:, 0:128], onesbf[0:FQK, :], kvr[:, 1024:1025]),
                                   deps=[rm1, const_tok])
                        rk2 = p.do("dve", lambda e: e.tensor_copy(ksrh[:], kstmp[:, 0:128]), deps=[rk1])
                        rk3 = p.do("dve", lambda e: e.tensor_sub(kstmp[:, 128:256], kstmp[:, 0:128], ksrh[:]), deps=[rk2])
                        rk4 = p.do("dve", lambda e: e.tensor_copy(ksrl[:], kstmp[:, 128:256]), deps=[rk3])
                        bdone = rm4
                        bdone2 = rk4
                        return bdone, bdone2
                bdone, bdone2 = phase_b1()
                p.barrier()

                def phase_b2a():
                    p.label = "b2a"
                    with ExitStack() as b2:
                        b2b = lambda n, s, d=F32: b2.enter_context(nc.sbuf_tensor("b2_" + n, s, d))
                        xt = b2b("xt", [128, DC * TT])
                        ut = b2b("ut", [128, DC * TT])
                        ath = b2b("ath", [128, DC * TT], BF16)
                        atl = b2b("atl", [128, DC * TT], BF16)
                        tmp = [b2b(f"tm{i}", [128, TT]) for i in range(3)]
                        den2 = b2b("den2", [128, TT])
                        rec = b2b("rec", [128, TT])
                        wsl = [b2b(f"ws{i}", [128, 2048], BF16) for i in range(2)]
                        pco = [b2.enter_context(nc.psum_tensor(f"b2_pc{i}", [128, TT], F32)) for i in range(4)]

                        prev_pe = [None]; prev_dve = [None]
                        slot_free = [None, None]
                        bank_free = [None] * 4
                        for tau in range(NTILE):
                            xtok = []
                            for c in range(DC):
                                tok = p.do("sp", lambda e, c=c, tau=tau:
                                           e.dma_start(xt[:, c * TT:(c + 1) * TT],
                                                       src.ap()[c * 128:(c + 1) * 128, tau * TT:(tau + 1) * TT]),
                                           deps=[prev_pe[0], prev_dve[0]], dma=True)
                                xtok.append(tok)
                            utok = []
                            for c in range(DC):
                                a = p.do("dve", lambda e, c=c, tau=tau:
                                         e.tensor_tensor(tmp[0][:], xt[:, c * TT:(c + 1) * TT],
                                                         mu_all[:, tau * TT:(tau + 1) * TT], ALU.subtract),
                                         deps=[xtok[-1], prev_pe[0]])
                                utok.append(p.do("dve", lambda e, c=c, tau=tau:
                                                 e.tensor_tensor(ut[:, c * TT:(c + 1) * TT], tmp[0][:],
                                                                 r0_all[:, tau * TT:(tau + 1) * TT], ALU.mult),
                                                 deps=[a]))
                            gd = None
                            d0 = [bdone, bdone2, bank_free[3]]
                            dml = [(ksrh, qhi), (ksrh, qlo), (ksrl, qhi)]
                            for i_, (wv, qv) in enumerate(dml):
                                gd = p.do("pe", lambda e, wv=wv, qv=qv, tau=tau, st=(i_ == 0), sp=(i_ == 2):
                                          e.matmul(pco[3][:], wv[:], qv[:, tau * TT:(tau + 1) * TT], start=st, stop=sp),
                                          deps=d0, inc=(i_ == 2))
                                d0 = ()
                            dd = p.do("dve", lambda e, tau=tau: e.tensor_tensor(den2[:], den_all[:, tau * TT:(tau + 1) * TT],
                                                                                pco[3][:], ALU.add), deps=[gd])
                            bank_free[3] = dd
                            rcp = p.do("dve", lambda e: e.reciprocal(rec[:], den2[:]), deps=[dd])
                            a_tok = None
                            for j in range(DC):
                                bank = j % 2
                                gc = None
                                d0 = [bdone, bank_free[bank]]
                                for i_, (wv, qv) in enumerate(((kvrh, qhi), (kvrh, qlo), (kvrl, qhi))):
                                    gc = p.do("pe", lambda e, j=j, wv=wv, qv=qv, tau=tau, bank=bank, st=(i_ == 0), sp=(i_ == 2):
                                              e.matmul(pco[bank][:], wv[:, j * 128:(j + 1) * 128],
                                                       qv[:, tau * TT:(tau + 1) * TT], start=st, stop=sp),
                                              deps=d0, inc=(i_ == 2))
                                    d0 = ()
                                s1_ = p.do("dve", lambda e, j=j, tau=tau, bank=bank:
                                           e.tensor_tensor(tmp[1][:], numb[:, j * TCORE + tau * TT: j * TCORE + (tau + 1) * TT],
                                                           pco[bank][:], ALU.add), deps=[gc])
                                s2_ = p.do("dve", lambda e: e.tensor_tensor(tmp[2][:], tmp[1][:], rec[:], ALU.mult), deps=[s1_, rcp])
                                h_ = p.do("dve", lambda e, j=j: e.tensor_copy(ath[:, j * TT:(j + 1) * TT], tmp[2][:]), deps=[s2_])
                                d_ = p.do("dve", lambda e, j=j: e.tensor_sub(tmp[1][:], tmp[2][:], ath[:, j * TT:(j + 1) * TT]), deps=[h_])
                                a_tok = p.do("dve", lambda e, j=j: e.tensor_copy(atl[:, j * TT:(j + 1) * TT], tmp[1][:]), deps=[d_])
                                bank_free[bank] = s1_
                            last_pe = None
                            last_dve = None
                            for m in range(DC):
                                slot = m % 2
                                wtok1 = p.do("sp", lambda e, m=m, slot=slot, w=wh["ow"]:
                                             e.dma_start(wsl[slot][:, 0:DC * 128], w.ap()[m * 128:(m + 1) * 128, :]),
                                             deps=[slot_free[slot]], dma="qw%d" % slot)
                                wtok2 = p.do("sp", lambda e, m=m, slot=slot, w=wl["ow"]:
                                             e.dma_start(wsl[slot][:, DC * 128:2 * DC * 128], w.ap()[m * 128:(m + 1) * 128, :]),
                                             dma="qw%d" % slot)
                                bank = 2 + (m % 2)
                                g_ = None
                                d0 = [wtok2, a_tok, bank_free[bank]]
                                idx = 0
                                for k in range(DC):
                                    for wo, mv in ((0, ath), (0, atl), (DC * 128, ath)):
                                        idx += 1
                                        il = idx == 3 * DC
                                        g_ = p.do("pe", lambda e, k=k, wo=wo, mv=mv, slot=slot, bank=bank, st=(idx == 1), sp=il:
                                                  e.matmul(pco[bank][:], wsl[slot][:, wo + k * 128:wo + (k + 1) * 128],
                                                           mv[:, k * TT:(k + 1) * TT], start=st, stop=sp),
                                                  deps=d0, inc=il)
                                        d0 = ()
                                slot_free[slot] = g_
                                s_ = p.do("dve", lambda e, m=m, tau=tau, bank=bank:
                                          e.scalar_tensor_tensor(numb[:, m * TCORE + tau * TT: m * TCORE + (tau + 1) * TT],
                                                                 pco[bank][:], scs["ob"][:, m:m + 1],
                                                                 ut[:, m * TT:(m + 1) * TT], ALU.add, ALU.add),
                                          deps=[g_, utok[m]])
                                bank_free[bank] = s_
                                last_pe = g_
                                last_dve = s_
                            prev_pe[0] = last_pe
                            prev_dve[0] = last_dve
                    p.barrier()
                phase_b2a()
                mid.close()

                def phase_b2b():
                    p.label = "b2b"
                    with ExitStack() as b3:
                        b3b = lambda n, s, d=F32: b3.enter_context(nc.sbuf_tensor("b3_" + n, s, d))
                        xs2 = [b3b(f"xs{i}", [128, TT]) for i in range(2)]
                        n2h = b3b("n2h", [128, DC * TT], BF16)
                        n2l = b3b("n2l", [128, DC * TT], BF16)
                        ph = b3b("ph", [128, HC * TT], BF16)
                        pl = b3b("pl", [128, HC * TT], BF16)
                        sq = [b3b("sq0", [128, TT])]
                        sq.append(sq[0])
                        tmp = [b3b(f"tm{i}", [128, TT]) for i in range(2)]
                        mu = b3b("mu", [128, TT]); ex2 = b3b("ex2", [128, TT])
                        veps = b3b("veps", [128, TT])
                        sqb = ex2; y0 = b3b("y0", [128, TT])
                        rb_ = b3b("rb", [128, TT])
                        gel = [b3b("ge0", [128, TT])]
                        gel.append(gel[0])
                        xo = [b3b("xo0", [128, TT])]
                        xo.append(xo[0])
                        wsl = [b3b(f"ws{i}", [128, 8192], BF16) for i in range(2)]
                        pst = b3.enter_context(nc.psum_tensor("b3_pst", [128, 1024], F32))
                        pmm = [b3.enter_context(nc.psum_tensor(f"b3_pm{i}", [128, TT], F32)) for i in range(4)]
                        lnbufs = (sq, tmp, mu, ex2, veps, sqb, y0, rb_)

                        stats_read = [None]
                        n2_free = [None]
                        slot_free = [None, None]
                        bank_free = [None] * 4
                        gel_free = [None, None]
                        xo_free = [None, None]
                        xs_free = [None, None]
                        n_toks = [None] * NTILE

                        def ln2_tile(tau):
                            t_mu, t_r, t_v, _ = emit_ln_stats(
                                lnbufs, lambda c, tau=tau: numb[:, c * TCORE + tau * TT: c * TCORE + (tau + 1) * TT],
                                pst, lambda c: [], stats_read[0])
                            stats_read[0] = t_v
                            n_tok = None
                            for c in range(DC):
                                a = p.do("dve", lambda e, c=c, tau=tau: e.tensor_tensor(
                                    tmp[0][:], numb[:, c * TCORE + tau * TT: c * TCORE + (tau + 1) * TT], mu[:], ALU.subtract),
                                         deps=[t_mu, t_r])
                                b = p.do("dve", lambda e: e.tensor_tensor(tmp[1][:], tmp[0][:], rb_[:], ALU.mult), deps=[a])
                                b2_ = p.do("dve", lambda e, c=c: e.tensor_scalar(tmp[0][:], tmp[1][:], scs["ln2w"][:, c:c + 1],
                                                                                 scs["ln2b"][:, c:c + 1], ALU.mult, ALU.add), deps=[b])
                                h_ = p.do("dve", lambda e, c=c: e.tensor_copy(n2h[:, c * TT:(c + 1) * TT], tmp[0][:]),
                                          deps=[b2_, n2_free[0]])
                                d_ = p.do("dve", lambda e, c=c: e.tensor_sub(tmp[1][:], tmp[0][:], n2h[:, c * TT:(c + 1) * TT]), deps=[h_])
                                n_tok = p.do("dve", lambda e, c=c: e.tensor_copy(n2l[:, c * TT:(c + 1) * TT], tmp[1][:]), deps=[d_])
                            return n_tok

                        n_toks[0] = ln2_tile(0)
                        for tau in range(NTILE):
                            n_tok = n_toks[tau]
                            p_done = None
                            for m in range(HC):
                                slot = m % 2
                                wtok1 = p.do("sp", lambda e, m=m, slot=slot, w=wh["f1"]:
                                             e.dma_start(wsl[slot][:, 0:DC * 128], w.ap()[m * 128:(m + 1) * 128, :]),
                                             deps=[slot_free[slot]], dma="qw%d" % slot)
                                wtok2 = p.do("sp", lambda e, m=m, slot=slot, w=wl["f1"]:
                                             e.dma_start(wsl[slot][:, DC * 128:2 * DC * 128], w.ap()[m * 128:(m + 1) * 128, :]),
                                             dma="qw%d" % slot)
                                bank = m % 4
                                g_ = None
                                d0 = [wtok2, n_tok, bank_free[bank]]
                                idx = 0
                                for k in range(DC):
                                    for wo, mv in ((0, n2h), (0, n2l), (DC * 128, n2h)):
                                        idx += 1
                                        il = idx == 3 * DC
                                        g_ = p.do("pe", lambda e, k=k, wo=wo, mv=mv, slot=slot, bank=bank, st=(idx == 1), sp=il:
                                                  e.matmul(pmm[bank][:], wsl[slot][:, wo + k * 128:wo + (k + 1) * 128],
                                                           mv[:, k * TT:(k + 1) * TT], start=st, stop=sp),
                                                  deps=d0, inc=il)
                                        d0 = ()
                                slot_free[slot] = g_
                                gi = m % 2
                                a1 = p.do("act", lambda e, m=m, gi=gi, bank=bank:
                                          e.activation(gel[gi][:], pmm[bank][:], AF.Gelu, bias=scs["f1b"][:, m:m + 1], scale=1.0),
                                          deps=[g_, gel_free[0], gel_free[1]])
                                h_ = p.do("dve", lambda e, m=m, gi=gi: e.tensor_copy(ph[:, m * TT:(m + 1) * TT], gel[gi][:]),
                                          deps=[a1])
                                d_ = p.do("dve", lambda e, m=m, gi=gi: e.tensor_sub(tmp[1][:], gel[gi][:], ph[:, m * TT:(m + 1) * TT]), deps=[h_])
                                p_done = p.do("dve", lambda e, m=m: e.tensor_copy(pl[:, m * TT:(m + 1) * TT], tmp[1][:]), deps=[d_])
                                gel_free[gi] = d_
                                bank_free[bank] = a1
                            n2_free[0] = slot_free[(HC - 1) % 2]

                            if tau + 1 < NTILE:
                                n_toks[tau + 1] = ln2_tile(tau + 1)

                            for m in range(DC):
                                slot = m % 2
                                wtok1 = p.do("sp", lambda e, m=m, slot=slot, w=wh["f2"]:
                                             e.dma_start(wsl[slot][:, 0:HC * 128], w.ap()[m * 128:(m + 1) * 128, :]),
                                             deps=[slot_free[slot]], dma="qw%d" % slot)
                                wtok2 = p.do("sp", lambda e, m=m, slot=slot, w=wl["f2"]:
                                             e.dma_start(wsl[slot][:, HC * 128:2 * HC * 128], w.ap()[m * 128:(m + 1) * 128, :]),
                                             dma="qw%d" % slot)
                                bank = m % 4
                                g_ = None
                                d0 = [wtok2, p_done, bank_free[bank]]
                                idx = 0
                                for k in range(HC):
                                    for wo, mv in ((0, ph), (0, pl), (HC * 128, ph)):
                                        idx += 1
                                        il = idx == 3 * HC
                                        g_ = p.do("pe", lambda e, k=k, wo=wo, mv=mv, slot=slot, bank=bank, st=(idx == 1), sp=il:
                                                  e.matmul(pmm[bank][:], wsl[slot][:, wo + k * 128:wo + (k + 1) * 128],
                                                           mv[:, k * TT:(k + 1) * TT], start=st, stop=sp),
                                                  deps=d0, inc=il)
                                        d0 = ()
                                slot_free[slot] = g_
                                xi = m % 2
                                xm_tok = p.do("sp", lambda e, m=m, tau=tau, xi=xi:
                                              e.dma_start(xs2[xi][:], src.ap()[m * 128:(m + 1) * 128, tau * TT:(tau + 1) * TT]),
                                              deps=[xs_free[xi]], dma=True)
                                s1_ = p.do("dve", lambda e, m=m, tau=tau, bank=bank:
                                           e.scalar_tensor_tensor(tmp[0][:], pmm[bank][:], scs["f2b"][:, m:m + 1],
                                                                  numb[:, m * TCORE + tau * TT: m * TCORE + (tau + 1) * TT],
                                                                  ALU.add, ALU.add),
                                           deps=[g_])
                                s2_ = p.do("dve", lambda e, m=m, xi=xi: e.tensor_tensor(xo[xi][:], xs2[xi][:],
                                                                                        tmp[0][:], ALU.add),
                                           deps=[s1_, xo_free[0], xo_free[1], xm_tok])
                                st_ = p.do("act", lambda e, m=m, tau=tau, xi=xi:
                                           e.dma_start(dst.ap()[m * 128:(m + 1) * 128, tau * TT:(tau + 1) * TT], xo[xi][:]),
                                           deps=[s2_], dma=True)
                                xo_free[xi] = st_
                                xs_free[xi] = s2_
                                bank_free[bank] = s1_

                phase_b2b()

        has_B = "B" in stages
        has_C = "C" in stages
        if "A" in stages:
            dstA = xres if (has_B or has_C) else y_out
            emit_ffn(x_in, dstA, "fA_")
            p.barrier()
        if has_B:
            srcB = xres if "A" in stages else x_in
            dstB = xres if has_C else y_out
            emit_attn(srcB, dstB)
            p.barrier()
        if has_C:
            srcC = xres if ("A" in stages or has_B) else x_in
            emit_ffn(srcC, y_out, "fC_")
            p.barrier()

        block = top.enter_context(nc.Block())
        p.replay(block, sems, nc)
    return nc



def tile_w(w):
    out_d, in_d = w.shape
    nm, nk = out_d // 128, in_d // 128
    wt = np.ascontiguousarray(np.asarray(w).T)
    a = wt.reshape(nk, 128, nm, 128)
    a = a.transpose(2, 1, 0, 3)
    return np.ascontiguousarray(a.reshape(nm * 128, nk * 128), dtype=np.float32)


def tile_qk(w):
    wt = np.ascontiguousarray(np.asarray(w).T)
    a = wt.reshape(8, 128, 64).transpose(1, 0, 2)
    return np.ascontiguousarray(a.reshape(128, 8 * 64), dtype=np.float32)


def sc_t(v):
    v = np.asarray(v)
    n = v.shape[0] // 128
    return np.ascontiguousarray(v.reshape(n, 128).T, dtype=np.float32)


def host_inputs(ins):
    g = {k: np.asarray(v) for k, v in ins.items()}
    shared = {
        "w1f": tile_w(g["L0_ffn_w1_w"]), "w2f": tile_w(g["L0_ffn_w2_w"]),
        "h1f": tile_w(g["L0_ffn_h1_w"]), "h2f": tile_w(g["L0_ffn_h2_w"]),
        "w3f": tile_w(g["L0_ffn_w3_w"]),
        "vwf": tile_w(g["L0_attn_v_w"]), "owf": tile_w(g["L0_attn_o_w"]),
        "f1f": tile_w(g["L0_attn_f1_w"]), "f2f": tile_w(g["L0_attn_f2_w"]),
        "qwf": tile_qk(g["L0_attn_q_w"]), "kwf": tile_qk(g["L0_attn_k_w"]),
        "s1_t": sc_t(g["L0_ffn_w1_s"]), "s2_t": sc_t(g["L0_ffn_w2_s"]),
        "sh1_t": sc_t(g["L0_ffn_h1_s"]), "sh2_t": sc_t(g["L0_ffn_h2_s"]),
        "s3_t": sc_t(g["L0_ffn_w3_s"]),
        "f1b_t": sc_t(g["L0_attn_f1_b"]),
        "vb_t": sc_t(g["L0_attn_v_b"]), "ob_t": sc_t(g["L0_attn_o_b"]),
        "f2b_t": sc_t(g["L0_attn_f2_b"]),
        "ln1w_t": sc_t(g["L0_attn_ln1_w"]), "ln1b_t": sc_t(g["L0_attn_ln1_b"]),
        "ln2w_t": sc_t(g["L0_attn_ln2_w"]), "ln2b_t": sc_t(g["L0_attn_ln2_b"]),
        "qb_t": np.ascontiguousarray(g["L0_attn_q_b"].reshape(64, 1), dtype=np.float32),
        "kb_t": np.ascontiguousarray(g["L0_attn_k_b"].reshape(64, 1), dtype=np.float32),
        "ident": np.eye(128, dtype=ml_dtypes.bfloat16),
        "maskT": np.triu(np.ones((128, 128))).astype(ml_dtypes.bfloat16),
        "ones32": np.ones((128, 128), dtype=np.float32),
        "onesbf": np.ones((128, 128), dtype=ml_dtypes.bfloat16),
    }
    x = np.asarray(g["L0_attn_x"])
    in_maps = []
    for core in range(8):
        b, h = core // 2, core % 2
        xs = np.ascontiguousarray(x[b, h * TCORE:(h + 1) * TCORE, :].T, dtype=np.float32)
        m = dict(shared)
        m["x"] = xs
        m["smask"] = np.full((64, 1), 1.0 if h == 0 else 0.0, np.float32)
        m["rmask"] = np.full((64, 1), 1.0 if h == 1 else 0.0, np.float32)
        in_maps.append(m)
    return in_maps


def gather(results):
    y = np.zeros((4, 4096, 1024), np.float32)
    for core in range(8):
        b, h = core // 2, core % 2
        y[b, h * TCORE:(h + 1) * TCORE, :] = results[core]["y"].T
    return y


_NC_CACHE = [None]


def kernel(**inputs):
    in_maps = host_inputs(inputs)
    if _NC_CACHE[0] is None:
        _NC_CACHE[0] = build()
    res = run_bass_kernel_spmd(_NC_CACHE[0], in_maps, core_ids=list(range(8)))
    return gather(res.results)
